# version 28
# baseline (speedup 1.0000x reference)
import numpy as np
import ml_dtypes
import concourse.bass as bass
import concourse.mybir as mybir
from concourse.bass_utils import run_bass_kernel_spmd
from contextlib import ExitStack

F32 = mybir.dt.float32
BF16 = mybir.dt.bfloat16
F8 = mybir.dt.float8e4
U8 = mybir.dt.uint8
AF = mybir.ActivationFunctionType
ALU = mybir.AluOpType
AX = mybir.AxisListType

NCORE = 8
TCORE = 2048
HT = 1024
NT = 8
D = 2048
E = 4096
NP = 32
NG = 8
ND = 16
NDC = 4
EPS = 1e-5
RD = 1.0 / 2048.0

ENGINES = ("sync", "scalar", "vector", "gpsimd", "tensor")
HALVES = 2


class Em:

    def __init__(self, marks=None, marksem=None):
        self.final = marks is not None
        self.marks = {} if marks is None else marks
        self.marksem = {} if marksem is None else marksem
        self.cnt = {}
        self.phases = []
        self.cur = None

    def phase(self, name):
        self.cur = {e: [] for e in ENGINES}
        self.phases.append((name, self.cur))

    def op(self, eng, fn, sem=None, mark=None, k=1):
        n = None
        if sem is not None:
            self.cnt[sem] = self.cnt.get(sem, 0) + k
            n = self.cnt[sem]
            if mark is not None:
                if not self.final:
                    self.marks[mark] = n
                    self.marksem[mark] = sem
        if self.final:
            self.cur[eng].append(("op", fn, sem, k))

    def wait(self, eng, sem, mark=None, value=None):
        if self.final:
            if value is None:
                assert mark in self.marks, f"undefined mark {mark}"
                assert self.marksem[mark] == sem, (
                    f"mark {mark} is on sem {self.marksem[mark]}, waited on {sem}")
            v = value if value is not None else self.marks[mark]
            if v > 0:
                self.cur[eng].append(("wait", sem, v))

    def at(self, mark):
        return self.marks.get(mark, 0)


def emit(e, t):

    XnT = t["XnT"]
    ATa = t["ATa"]
    WAa = t["WAa"]
    F32A = t["F32A"]
    ybuf = t["ybuf"]
    o1e = ybuf.bitcast(BF16)
    SCR = t["SCR"]
    SCRb = SCR.bitcast(BF16)
    SCR8 = SCR.bitcast(F8)
    gbf = t["gbf"]; bbf = t["bbf"]
    dtw = t["dtw"]
    dtc = t["dtc"]
    cst = t["cst"]
    sm = t["sm"]
    scb = t["scb"]
    st = t["st"]
    P = (t["P0"], t["P1"])

    x = t["x"]; grep = t["grep"]; brep = t["brep"]
    win = t["win"]; wout = t["wout"]; w8 = t["w8"]
    dtrep = t["dtrep"]; out = t["out"]
    winb = t["winb"]; woutb = t["woutb"]; w3b = t["w3b"]
    xnb = t["xnb"]; o1b = t["o1b"]; yhb = t["yhb"]
    wscol = t["wscol"]

    def zs(par):
        return WAa[:, par * 8192:(par + 1) * 8192].rearrange("p (k e) -> p k e", k=16)

    def gs(par):
        return WAa[:, 16384 + par * 8192:16384 + (par + 1) * 8192].rearrange(
            "p (k e) -> p k e", k=16)

    def wo(par):
        return WAa[:, par * 16384:(par + 1) * 16384].rearrange("p (k e) -> p k e", k=32)

    w3T = WAa[:, 0:32768].rearrange("p (k o) -> p k o", k=16)

    def ATl(et):
        return ATa[:, et * 1024:(et + 1) * 1024]

    yhT = ATa[:, 0:8192].rearrange("p (u k c) -> p u k c", u=2, k=16)
    o1t = ATa[:, 8192:24576].rearrange("p (u c) -> p u c", u=8)
    lob = ATa[:, 24576:26624]
    hib = ATa[:, 26624:28672]
    yh = ATa[:, 28672:30720]
    xnt_stage = ATa[:, 30720:32768]

    zcb = SCRb[:, 0:1024]
    gcb = SCRb[:, 1024:2048]
    zgb = SCRb[:, 2048:3072]
    qlo = SCR8[:, 0:2048]
    qhi = SCR8[:, 2048:4096]
    msk = SCR[:, 4096:6144]

    xs = (F32A[:, 0, :], F32A[:, 1, :])
    txg = F32A[:, 2, :]
    x1s = (F32A[:, 0, :], F32A[:, 1, :], F32A[:, 2, :])
    dtrep_s = XnT.bitcast(F32).rearrange(
        "p a b -> p (a b)")[:, 0:4096]

    def f(ti, j):
        return st[:, ti, j:j + 1]

    e.phase("S")
    e.op("scalar", lambda E: E.dma_start(dtc[:, :], t["dtcol"][:, :]), "stp", k=16)
    e.op("scalar", lambda E: E.dma_start(scb[:, :], t["scol"][:, :]), "stp", k=16)
    e.op("scalar", lambda E: E.dma_start(cst[:, 3:4], wscol[:, :]), "stp", "ld_all", k=16)
    e.wait("vector", "stp", "ld_all")
    e.op("vector", lambda E: E.tensor_copy(cst[:, 2:3], scb[:, :]))
    e.op("vector", lambda E: E.drain())
    e.op("vector", lambda E: E.reciprocal(cst[:, 0:1], cst[:, 2:3]))
    e.op("vector", lambda E: E.drain())
    e.op("vector", lambda E: E.tensor_tensor(cst[:, 1:2], cst[:, 2:3], cst[:, 3:4], ALU.mult),
         "dve", "consts")
    e.op("gpsimd", lambda E: E.dma_start(dtrep_s[:, :], dtrep[:, :]), "cast", "ld_dt", k=16)
    e.wait("vector", "cast", "ld_dt")
    e.op("vector", lambda E: E.tensor_scalar(dtrep_s[:, :], dtrep_s[:, :], -1.0, None, ALU.mult))
    e.op("vector", lambda E: E.drain())
    e.op("vector", lambda E: E.reduce_max(sm[:, 0:1], dtrep_s[:, :], axis=AX.X))
    e.op("vector", lambda E: E.drain())
    e.op("vector", lambda E: E.tensor_scalar(sm[:, 1:2], sm[:, 0:1], -1.0, None, ALU.mult),
         "dve", "negm")
    e.wait("scalar", "dve", "negm")
    e.op("scalar", lambda E: E.activation(dtrep_s[:, :], dtrep_s[:, :], AF.Exp,
                                          bias=sm[:, 1:2]), "act", "expden")
    e.wait("vector", "act", "expden")
    e.op("vector", lambda E: E.reduce_sum(sm[:, 2:3], dtrep_s[:, :], axis=AX.X))
    e.op("vector", lambda E: E.drain())
    e.op("vector", lambda E: E.reciprocal(sm[:, 3:4], sm[:, 2:3]), "dve", "rs")
    e.wait("scalar", "stp", "ld_all")
    e.op("scalar", lambda E: E.activation(dtw[:, :], dtc[:, :], AF.Exp,
                                          scale=-1.0, bias=sm[:, 1:2]), "act", "expcol")
    e.wait("vector", "act", "expcol")
    e.op("vector", lambda E: E.drain())
    e.op("vector", lambda E: E.tensor_scalar(dtw[:, :], dtw[:, :], sm[:, 3:4], None,
                                             ALU.mult), "dve", "dtw")
    e.op("gpsimd", lambda E: E.dma_start(gbf[:, :], grep[:, :]), "cast", "cgb", k=16)
    e.op("gpsimd", lambda E: E.dma_start(bbf[:, :], brep[:, :]), "cast", "cbb", k=16)
    def emit_cast_zg(g):
        e.op("gpsimd", lambda E, g=g: E.dma_start(
            winb[512 * g:512 * (g + 1), :], win[512 * g:512 * (g + 1), :]),
            "cast", f"cz{g}", k=16)
        e.op("gpsimd", lambda E, g=g: E.dma_start(
            winb[4096 + 512 * g:4096 + 512 * (g + 1), :],
            win[4096 + 512 * g:4096 + 512 * (g + 1), :]), "cast", f"cg{g}", k=16)

    def emit_cast_rest():
        for g in range(1, NG):
            emit_cast_zg(g)
        for c in range(NDC):
            e.op("gpsimd", lambda E, c=c: E.dma_start(
                woutb[512 * c:512 * (c + 1), :], wout[512 * c:512 * (c + 1), :]),
                "cast", f"cw{c}", k=16)
        for c in range(2):
            e.op("gpsimd", lambda E, c=c: E.dma_start(
                w3b[1024 * c:1024 * (c + 1), :], w8[1024 * c:1024 * (c + 1), :]),
                "cast", f"c3{c}", k=16)

    emit_cast_zg(0)

    for h in range(HALVES):
        hb = h * NT
        r0 = h * HT

        e.phase(f"A{h}")
        for i in range(NT):
            ti = hb + i
            if h == 1 and i < 2:
                e.wait("scalar", "out0", value=e.at(f"outw{0}_{6}"))
                e.wait("scalar", "out1", value=e.at(f"outw{0}_{7}"))
                e.wait("scalar", "out2", value=e.at(f"outw{0}_{5}"))
            if i >= 2:
                e.wait("scalar", "act", f"hAa{h}_{i-2}")
                e.wait("scalar", "dve", f"hAd{h}_{i-2}")
            e.op("scalar", lambda E, i=i, r0=r0: E.dma_start(
                xs[i % 2][:, :], x[r0 + i * 128:r0 + (i + 1) * 128, :]),
                f"ld{i % 2}", f"ldxA{h}_{i}", k=16)
            e.wait("scalar", f"ld{i % 2}", f"ldxA{h}_{i}")
            e.op("scalar", lambda E, i=i, ti=ti: E.activation(
                txg[:, :], xs[i % 2][:, :], AF.Square, accum_out=f(ti, 2)),
                "act", f"hAa{h}_{i}")
            e.wait("vector", f"ld{i % 2}", f"ldxA{h}_{i}")
            e.op("vector", lambda E, i=i, ti=ti: E.reduce_sum(
                f(ti, 0), xs[i % 2][:, :], axis=AX.X), "dve", f"hAd{h}_{i}")
            e.op("vector", lambda E: E.drain())

        def bv(fn, sem=None, mark=None, hb=hb):
            for q in range(NT):
                last = q == NT - 1
                e.op("vector", (lambda E, q=q, hb=hb: fn(E, hb + q)),
                     sem if last else None, mark if last else None)
            e.op("vector", lambda E: E.drain())

        def ba(fn, sem=None, mark=None, hb=hb):
            for q in range(NT):
                last = q == NT - 1
                e.op("scalar", (lambda E, q=q, hb=hb: fn(E, hb + q)),
                     sem if last else None, mark if last else None)

        e.wait("vector", "act", f"hAa{h}_{NT-1}")
        bv(lambda E, ti: E.tensor_scalar(f(ti, 1), f(ti, 0), RD, None, ALU.mult))
        bv(lambda E, ti: E.tensor_scalar(f(ti, 3), f(ti, 2), RD, None, ALU.mult))
        bv(lambda E, ti: E.tensor_tensor(f(ti, 4), f(ti, 1), f(ti, 1), ALU.mult))
        bv(lambda E, ti: E.tensor_tensor(f(ti, 5), f(ti, 3), f(ti, 4), ALU.subtract))
        bv(lambda E, ti: E.tensor_scalar(f(ti, 6), f(ti, 5), EPS, None, ALU.add),
           "dve", f"bA1{h}")
        e.wait("scalar", "dve", f"bA1{h}")
        ba(lambda E, ti: E.activation(f(ti, 7), f(ti, 6), AF.Sqrt), "act", f"bA1s{h}")
        e.wait("vector", "act", f"bA1s{h}")
        bv(lambda E, ti: E.reciprocal(f(ti, 8), f(ti, 7)))
        bv(lambda E, ti: E.tensor_tensor(f(ti, 4), f(ti, 8), f(ti, 8), ALU.mult))
        bv(lambda E, ti: E.tensor_tensor(f(ti, 4), f(ti, 4), f(ti, 6), ALU.mult))
        bv(lambda E, ti: E.tensor_scalar(f(ti, 4), f(ti, 4), -0.5, 1.5, ALU.mult, ALU.add))
        bv(lambda E, ti: E.tensor_tensor(f(ti, 8), f(ti, 8), f(ti, 4), ALU.mult))
        bv(lambda E, ti: E.tensor_scalar(f(ti, 9), f(ti, 8), 1.0, None, ALU.add))
        bv(lambda E, ti: E.scalar_tensor_tensor(
            f(ti, 10), f(ti, 1), -1.0, f(ti, 8), ALU.mult, ALU.mult))
        bv(lambda E, ti: E.tensor_tensor(f(ti, 11), f(ti, 8), f(ti, 8), ALU.mult))
        bv(lambda E, ti: E.tensor_tensor(f(ti, 11), f(ti, 11), f(ti, 5), ALU.mult))
        bv(lambda E, ti: E.tensor_scalar(f(ti, 11), f(ti, 11), EPS, None, ALU.add),
           "dve", f"bA2{h}")
        e.wait("scalar", "dve", f"bA2{h}")
        ba(lambda E, ti: E.activation(f(ti, 12), f(ti, 11), AF.Sqrt), "act", f"bA2s{h}")
        e.wait("vector", "act", f"bA2s{h}")
        bv(lambda E, ti: E.reciprocal(f(ti, 13), f(ti, 12)))
        bv(lambda E, ti: E.tensor_tensor(f(ti, 4), f(ti, 13), f(ti, 13), ALU.mult))
        bv(lambda E, ti: E.tensor_tensor(f(ti, 4), f(ti, 4), f(ti, 11), ALU.mult))
        bv(lambda E, ti: E.tensor_scalar(f(ti, 4), f(ti, 4), -0.5, 1.5, ALU.mult, ALU.add))
        bv(lambda E, ti: E.tensor_tensor(f(ti, 13), f(ti, 13), f(ti, 4), ALU.mult))
        bv(lambda E, ti: E.tensor_tensor(f(ti, 14), f(ti, 8), f(ti, 13), ALU.mult))
        bv(lambda E, ti: E.tensor_tensor(f(ti, 15), f(ti, 10), f(ti, 13), ALU.mult),
           "dve", f"bA{h}")

        tg = (txg, ybuf)
        xst = (xnt_stage, yh)
        for i in range(NT):
            ti = hb + i
            if i < 2:
                e.wait("scalar", "dve", f"bA{h}")
            else:
                e.wait("scalar", "act", f"tA{h}_{i-2}")
            e.op("scalar", lambda E, i=i, r0=r0: E.dma_start(
                xs[i % 2][:, :], x[r0 + i * 128:r0 + (i + 1) * 128, :]),
                f"ld{i % 2}", f"ldA2{h}_{i}", k=16)
            e.wait("scalar", f"ld{i % 2}", f"ldA2{h}_{i}")
            if i >= 2:
                e.wait("scalar", "gp", f"xnA{h}_{i-2}")
            e.op("scalar", lambda E, i=i, ti=ti: E.activation(
                tg[i % 2][:, :], xs[i % 2][:, :], AF.Identity,
                scale=f(ti, 14), bias=f(ti, 15)), "act", f"tA{h}_{i}")
            e.wait("vector", "act", f"tA{h}_{i}")
            if h == 0 and i == 0:
                e.wait("vector", "cast", "cgb")
            e.op("vector", lambda E, i=i: E.tensor_tensor(
                tg[i % 2][:, :], tg[i % 2][:, :], gbf[:, :], ALU.mult),
                "dve", f"xgA{h}_{i}")
            e.wait("gpsimd", "dve", f"xgA{h}_{i}")
            if h == 0 and i == 0:
                e.wait("gpsimd", "cast", "cbb")
            if i >= 2:
                e.wait("gpsimd", "xnw", f"xnw{h}_{i-2}")
            if h == 1 and i == 0:
                e.wait("gpsimd", "pe", f"peC{0}_{15}")
            e.op("gpsimd", lambda E, i=i: E.tensor_tensor(
                xst[i % 2][:, :], tg[i % 2][:, :], bbf[:, :], ALU.add),
                "gp", f"xnA{h}_{i}")
            e.wait("sync", "gp", f"xnA{h}_{i}")
            e.op("sync", lambda E, i=i: E.dma_start(
                xnb[i * 128:(i + 1) * 128, :], xst[i % 2][:, :]),
                "xnw", f"xnw{h}_{i}", k=16)
        if h == 0:
            emit_cast_rest()
        e.wait("scalar", "xnw", f"xnw{h}_{7}")
        if h == 0:
            e.wait("scalar", "dve", "rs")
        if h == 1:
            e.wait("scalar", "pe", f"peB{0}_{NP-1}")
        for dp in range(16):
            e.op("scalar", lambda E, dp=dp: E.dma_start(
                XnT[:, dp, :], xnb[0:HT, dp * 128:(dp + 1) * 128], transpose=True),
                "xnt", f"xnt{h}_{dp}", k=16)

        e.phase(f"B{h}")
        for g in range(NG):
            par = g % 2
            if h == 0:
                e.wait("sync", "cast", f"cz{g}")
                e.wait("sync", "cast", f"cg{g}")
            if h == 1 and g < 2:
                e.wait("sync", "pe", f"peD{0}_{7}")
            if g >= 2:
                e.wait("sync", "pe", f"peB{h}_{4*(g-2)+3}")
            for dp in range(16):
                e.op("sync", lambda E, g=g, dp=dp, par=par: E.dma_start(
                    zs(par)[:, dp, :],
                    winb[512 * g:512 * (g + 1), dp * 128:(dp + 1) * 128],
                    transpose=True), "wzA" if par == 0 else "wzB", k=16)
            for dp in range(16):
                e.op("sync", lambda E, g=g, dp=dp, par=par: E.dma_start(
                    gs(par)[:, dp, :],
                    winb[4096 + 512 * g:4096 + 512 * (g + 1), dp * 128:(dp + 1) * 128],
                    transpose=True), "wzA" if par == 0 else "wzB",
                    f"wz{h}_{g}" if dp == 15 else None, k=16)

        for p in range(NP):
            g = p // 4
            par = g % 2
            sl = (p % 4) * 128
            if p % 4 == 0:
                e.wait("tensor", "wzA" if par == 0 else "wzB", f"wz{h}_{g}")
            if p == 0:
                e.wait("tensor", "xnt", f"xnt{h}_{15}")
                e.wait("tensor", "dve", "dtw")
            pf = t["psum_free"].get((h, "B", p))
            if pf:
                e.wait("tensor", pf[0], value=pf[1])
            for k in range(16):
                for tc in range(2):
                    e.op("tensor", lambda E, p=p, k=k, tc=tc, par=par, sl=sl: E.matmul(
                        P[p % 2][:, tc * 512:(tc + 1) * 512],
                        zs(par)[:, k, sl:sl + 128],
                        XnT[:, k, tc * 512:(tc + 1) * 512],
                        start=(k == 0), stop=(k == 15)))
            for k in range(16):
                for tc in range(2):
                    e.op("tensor", lambda E, p=p, k=k, tc=tc, par=par, sl=sl: E.matmul(
                        P[p % 2][:, 1024 + tc * 512:1024 + (tc + 1) * 512],
                        gs(par)[:, k, sl:sl + 128],
                        XnT[:, k, tc * 512:(tc + 1) * 512],
                        start=(k == 0), stop=(k == 15)),
                        sem="pe" if (k == 15 and tc == 1) else None,
                        mark=f"peB{h}_{p}" if (k == 15 and tc == 1) else None)

            e.wait("vector", "pe", f"peB{h}_{p}")
            if p >= 1:
                e.wait("vector", "gp", f"mulB{h}_{p-1}")
            e.op("vector", lambda E, p=p: E.tensor_scalar(
                zcb[:, :], P[p % 2][:, 0:1024], -15.0, 15.0, ALU.max, ALU.min),
                "dve", f"czB{h}_{p}")
            e.op("vector", lambda E, p=p: E.tensor_scalar(
                gcb[:, :], P[p % 2][:, 1024:2048], -15.0, 15.0, ALU.max, ALU.min),
                "dve", f"cgB{h}_{p}")
            e.wait("vector", "act", f"sigB{h}_{p}")
            e.op("vector", lambda E, p=p: E.tensor_tensor(
                zgb[:, :], P[p % 2][:, 0:1024], zcb[:, :], ALU.mult),
                "dve", f"zgB{h}_{p}")
            e.op("vector", lambda E: E.drain())
            e.wait("scalar", "dve", f"czB{h}_{p}")
            e.op("scalar", lambda E: E.activation(zcb[:, :], zcb[:, :], AF.Sigmoid),
                 "act", f"sigB{h}_{p}")
            e.wait("scalar", "dve", f"cgB{h}_{p}")
            e.wait("scalar", "dve", f"zgB{h}_{p}")
            e.op("scalar", lambda E: E.activation(zcb[:, :], gcb[:, :], AF.Sigmoid),
                 "act", f"sigG{h}_{p}")
            e.wait("gpsimd", "act", f"sigG{h}_{p}")
            if h == 1 and p == 0:
                e.wait("gpsimd", "dve", f"selD{0}_{7}")
            e.op("gpsimd", lambda E: E.tensor_tensor(
                gcb[:, :], gcb[:, :], zcb[:, :], ALU.mult), "gp", f"mulB{h}_{p}")
            e.wait("vector", "gp", f"mulB{h}_{p}")
            if h == 1 and p == 0:
                e.wait("vector", "xnw", f"xnw{1}_{7}")
                e.wait("vector", "pe", f"peD{0}_{7}")
                e.wait("vector", "yhwB", f"yhw{0}_{7}")
            if h == 0 and p == 0:
                e.wait("vector", "dve", "dtw")
            e.op("vector", lambda E, p=p: E.scalar_tensor_tensor(
                ATl(p)[:, :], zgb[:, :], dtw[:, p:p + 1], gcb[:, :],
                ALU.mult, ALU.mult), "dve", f"sttB{h}_{p}")

        e.phase(f"C{h}")
        def emit_wo(c):
            par = c % 2
            q = "sync" if par == 0 else "scalar"
            if h == 0:
                e.wait(q, "cast", f"cw{c}")
            if c < 2:
                e.wait(q, "pe", f"peB{h}_{27 if c == 0 else 31}")
            else:
                e.wait(q, "pe", f"peC{h}_{4*(c-2)+3}")
            for ep in range(32):
                e.op(q, lambda E, c=c, ep=ep, par=par: E.dma_start(
                    wo(par)[:, ep, :],
                    woutb[512 * c:512 * (c + 1), ep * 128:(ep + 1) * 128],
                    transpose=True), "woA" if par == 0 else "woB",
                    f"wo{h}_{c}" if ep == 31 else None, k=16)

        emit_wo(0)
        emit_wo(1)
        emit_wo(2)
        for dt_ in range(ND):
            if dt_ == 8:
                emit_wo(3)
            c = dt_ // 4
            par = c % 2
            sl = (dt_ % 4) * 128
            if dt_ % 4 == 0:
                e.wait("tensor", "woA" if par == 0 else "woB", f"wo{h}_{c}")
            if dt_ == 0:
                e.wait("tensor", "dve", f"sttB{h}_{NP-1}")
            pf = t["psum_free"].get((h, "C", dt_))
            if pf:
                e.wait("tensor", pf[0], value=pf[1])
            for ep in range(32):
                for tc in range(2):
                    e.op("tensor", lambda E, dt_=dt_, ep=ep, tc=tc, par=par, sl=sl: E.matmul(
                        P[dt_ % 2][:, tc * 512:(tc + 1) * 512],
                        wo(par)[:, ep, sl:sl + 128],
                        ATl(ep)[:, tc * 512:(tc + 1) * 512],
                        start=(ep == 0), stop=(ep == 31)),
                        sem="pe" if (ep == 31 and tc == 1) else None,
                        mark=f"peC{h}_{dt_}" if (ep == 31 and tc == 1) else None)
            e.wait("vector", "pe", f"peC{h}_{dt_}")
            if dt_ >= 2:
                e.wait("vector", "o1wA" if dt_ % 2 == 0 else "o1wB", f"o1w{h}_{dt_-2}")
            e.op("vector", lambda E, dt_=dt_: E.tensor_copy(
                o1e[:, (dt_ % 2) * 1024:(dt_ % 2 + 1) * 1024], P[dt_ % 2][:, 0:1024]),
                "dve", f"o1eC{h}_{dt_}")
            e.wait("scalar", "dve", f"o1eC{h}_{dt_}")
            e.op("scalar", lambda E, dt_=dt_: E.dma_start(
                o1b[dt_ * 128:(dt_ + 1) * 128, :],
                o1e[:, (dt_ % 2) * 1024:(dt_ % 2 + 1) * 1024]),
                "o1wA" if dt_ % 2 == 0 else "o1wB", f"o1w{h}_{dt_}", k=16)

        e.phase(f"D{h}")
        e.wait("sync", "pe", f"peC{h}_{15}")
        e.wait("sync", "o1wA", f"o1w{h}_{14}")
        e.wait("sync", "o1wB", f"o1w{h}_{15}")
        for i in range(NT):
            e.op("sync", lambda E, i=i: E.dma_start(
                o1t[:, i, :], o1b[0:2048, i * 128:(i + 1) * 128], transpose=True),
                f"ot{i}", f"o1t{h}_{i}", k=16)
        if h == 0:
            e.wait("sync", "cast", "c31")
        for k in range(16):
            e.op("sync", lambda E, k=k: E.dma_start(
                w3T[:, k, :], w3b[0:2048, k * 128:(k + 1) * 128], transpose=True),
                "w3t", f"w3t{h}_{k}" if k == 15 else None, k=16)

        for i in range(NT):
            ti = hb + i
            if i == 0:
                e.wait("scalar", "act", f"tA{h}_{NT-2}")
            elif i == 1:
                e.wait("scalar", "act", f"tA{h}_{NT-1}")
            else:
                e.wait("scalar", "dve", f"hDd{h}_{i-2}")
                e.wait("scalar", "act", f"hDa{h}_{i-2}")
            e.op("scalar", lambda E, i=i, r0=r0: E.dma_start(
                xs[i % 2][:, :], x[r0 + i * 128:r0 + (i + 1) * 128, :]),
                f"ld{i % 2}", f"ldxD{h}_{i}", k=16)
            e.wait("scalar", f"ld{i % 2}", f"ldxD{h}_{i}")
            e.op("scalar", lambda E, i=i, ti=ti: E.activation(
                xs[i % 2][:, :], xs[i % 2][:, :], AF.Identity,
                scale=f(ti, 9), bias=f(ti, 10)), "act", f"xaD{h}_{i}")
            if i >= 1:
                e.wait("scalar", "dve", f"x1D{h}_{i-1}")
                e.op("scalar", lambda E, i=i, ti=ti: E.activation(
                    ybuf[:, :], xs[(i - 1) % 2][:, :], AF.Square,
                    accum_out=f(ti - 1, 16)), "act", f"hDa{h}_{i-1}")
            e.wait("vector", "act", f"xaD{h}_{i}")
            e.wait("vector", f"ot{i}", f"o1t{h}_{i}")
            e.op("vector", lambda E, i=i: E.tensor_tensor(
                xs[i % 2][:, :], xs[i % 2][:, :], o1t[:, i, :], ALU.add),
                "dve", f"x1D{h}_{i}")
            e.op("vector", lambda E: E.drain())
            e.op("vector", lambda E, i=i, ti=ti: E.reduce_sum(
                f(ti, 17), xs[i % 2][:, :], axis=AX.X), "dve", f"hDd{h}_{i}")
            e.op("vector", lambda E: E.drain())
        e.wait("scalar", "dve", f"x1D{h}_{NT-1}")
        e.op("scalar", lambda E, ti=hb + NT - 1: E.activation(
            ybuf[:, :], xs[(NT - 1) % 2][:, :], AF.Square,
            accum_out=f(ti, 16)), "act", f"hDa{h}_{NT-1}")

        def bv(fn, sem=None, mark=None, hb=hb):
            for q in range(NT):
                last = q == NT - 1
                e.op("vector", (lambda E, q=q, hb=hb: fn(E, hb + q)),
                     sem if last else None, mark if last else None)
            e.op("vector", lambda E: E.drain())

        def ba(fn, sem=None, mark=None, hb=hb):
            for q in range(NT):
                last = q == NT - 1
                e.op("scalar", (lambda E, q=q, hb=hb: fn(E, hb + q)),
                     sem if last else None, mark if last else None)

        e.wait("vector", "act", f"hDa{h}_{NT-1}")
        if h == 0:
            e.wait("vector", "dve", "consts")
        bv(lambda E, ti: E.tensor_scalar(f(ti, 18), f(ti, 17), RD, None, ALU.mult))
        bv(lambda E, ti: E.tensor_scalar(f(ti, 19), f(ti, 16), RD, None, ALU.mult))
        bv(lambda E, ti: E.tensor_tensor(f(ti, 4), f(ti, 18), f(ti, 18), ALU.mult))
        bv(lambda E, ti: E.tensor_tensor(f(ti, 19), f(ti, 19), f(ti, 4), ALU.subtract))
        bv(lambda E, ti: E.tensor_scalar(f(ti, 19), f(ti, 19), EPS, None, ALU.add),
           "dve", f"bD1{h}")
        e.wait("scalar", "dve", f"bD1{h}")
        ba(lambda E, ti: E.activation(f(ti, 5), f(ti, 19), AF.Sqrt), "act", f"bD1s{h}")
        e.wait("vector", "act", f"bD1s{h}")
        bv(lambda E, ti: E.reciprocal(f(ti, 6), f(ti, 5)))
        bv(lambda E, ti: E.tensor_tensor(f(ti, 4), f(ti, 6), f(ti, 6), ALU.mult))
        bv(lambda E, ti: E.tensor_tensor(f(ti, 4), f(ti, 4), f(ti, 19), ALU.mult))
        bv(lambda E, ti: E.tensor_scalar(f(ti, 4), f(ti, 4), -0.5, 1.5, ALU.mult, ALU.add))
        bv(lambda E, ti: E.tensor_tensor(f(ti, 6), f(ti, 6), f(ti, 4), ALU.mult))
        bv(lambda E, ti: E.tensor_scalar(f(ti, 7), f(ti, 6), cst[:, 0:1], None, ALU.mult))
        bv(lambda E, ti: E.scalar_tensor_tensor(
            f(ti, 8), f(ti, 18), -1.0, f(ti, 7), ALU.mult, ALU.mult), "dve", f"bD{h}")

        def emit_out(i):
            e.wait("vector", "pe", f"peD{h}_{i}")
            e.op("vector", lambda E, i=i: E.scalar_tensor_tensor(
                x1s[i % 3][:, :], P[i % 2][:, :], cst[:, 1:2], x1s[i % 3][:, :],
                ALU.mult, ALU.add), "dve", f"outD{h}_{i}")
            e.wait("scalar", "dve", f"outD{h}_{i}")
            e.op("scalar", lambda E, i=i, r0=r0: E.dma_start(
                out[r0 + i * 128:r0 + (i + 1) * 128, :], x1s[i % 3][:, :]),
                f"out{i % 3}", f"outw{h}_{i}", k=16)

        XnTb = XnT.rearrange("p a b -> p (a b)")
        XnT8 = XnT.bitcast(F8)
        XnTu = XnT.bitcast(U8)
        ybs = (ybuf, XnTb[:, 4 * 1024:8 * 1024].bitcast(F32))
        lobs = (lob, XnTb[:, 0:2048])
        hibs = (hib, XnTb[:, 2048:4096])
        yhs = (yh, XnTb[:, 12 * 1024:14 * 1024])
        msks = (msk, XnTu[:, 8, :])
        qlos = (qlo, XnT8[:, 9, :])
        qhis = (qhi, XnT8[:, 10, :])

        def qdq(j):
            q = j % 2
            u = j // 2
            e.wait("scalar", "dve", f"x1b{h}_{j}")
            if j >= 2:
                e.wait("scalar", "dve", f"upD{h}_{j-2}")
                e.wait("scalar", "dve", f"selD{h}_{j-2}")
            e.op("scalar", lambda E, j=j, ti=hb + j, q=q: E.activation(
                ybs[q][:, :], x1s[j % 3][:, :], AF.Identity,
                scale=f(ti, 7), bias=f(ti, 8)), "act", f"yD{h}_{j}")
            e.wait("vector", "act", f"yD{h}_{j}")
            e.op("vector", lambda E, q=q: E.tensor_scalar(
                ybs[q][:, :], ybs[q][:, :], -448.0, 448.0, ALU.max, ALU.min),
                "dve", f"vcD{h}_{j}")
            e.wait("scalar", "dve", f"vcD{h}_{j}")
            e.op("scalar", lambda E, q=q: E.activation(
                qlos[q][:, :], ybs[q][:, :], AF.Copy))
            e.op("scalar", lambda E, q=q: E.activation(
                qhis[q][:, :], ybs[q][:, :], AF.Copy, scale=0.5))
            e.op("scalar", lambda E: E.drain())
            e.op("scalar", lambda E, q=q: E.activation(
                msks[q][:, :], qlos[q][:, :], AF.Is_finite), "act", f"m8D{h}_{j}")
            e.wait("vector", "act", f"m8D{h}_{j}")
            e.op("vector", lambda E, q=q: E.tensor_copy(lobs[q][:, :], qlos[q][:, :]))
            e.op("vector", lambda E, q=q: E.tensor_scalar(
                hibs[q][:, :], qhis[q][:, :], 2.0, None, ALU.mult), "dve", f"upD{h}_{j}")
            e.op("vector", lambda E: E.drain())
            if j >= 2:
                e.wait("vector", "yhwA" if ((j - 2) // 2) % 2 == 0 else "yhwB",
                       f"yhw{h}_{j-2}")
            e.op("vector", lambda E, q=q: E.select(
                yhs[q][:, :], msks[q][:, :], lobs[q][:, :], hibs[q][:, :]),
                "dve", f"selD{h}_{j}")
            e.wait("scalar", "dve", f"selD{h}_{j}")
            e.op("scalar", lambda E, j=j, q=q: E.dma_start(
                yhb[j * 128:(j + 1) * 128, :], yhs[q][:, :]),
                "yhwA" if u % 2 == 0 else "yhwB", f"yhw{h}_{j}", k=16)
            if j % 2 == 1:
                e.wait("sync", "yhwA" if u % 2 == 0 else "yhwB", f"yhw{h}_{j}")
                if u >= 2:
                    e.wait("sync", "pe", f"peD{h}_{2*(u-2)+1}")
                for k in range(16):
                    e.op("sync", lambda E, u=u, k=k: E.dma_start(
                        yhT[:, u % 2, k, :],
                        yhb[(u * 256):(u * 256 + 256), k * 128:(k + 1) * 128],
                        transpose=True), "yhtA" if u % 2 == 0 else "yhtB",
                        f"yht{h}_{u}" if k == 15 else None, k=16)
            if j % 2 == 0:
                e.wait("tensor", "yhtA" if u % 2 == 0 else "yhtB", f"yht{h}_{u}")
            if j == 0:
                e.wait("tensor", "w3t", f"w3t{h}_{15}")
            pf = t["psum_free"].get((h, "D", j))
            if pf:
                e.wait("tensor", pf[0], value=pf[1])
            for k in range(16):
                for oc in range(4):
                    e.op("tensor", lambda E, j=j, u=u, k=k, oc=oc: E.matmul(
                        P[j % 2][:, oc * 512:(oc + 1) * 512],
                        yhT[:, u % 2, k, (j % 2) * 128:(j % 2) * 128 + 128],
                        w3T[:, k, oc * 512:(oc + 1) * 512],
                        start=(k == 0), stop=(k == 15)),
                        sem="pe" if (k == 15 and oc == 3) else None,
                        mark=f"peD{h}_{j}" if (k == 15 and oc == 3) else None)

        for i in range(NT):
            ti = hb + i
            if i < 3:
                e.wait("scalar", "dve", f"bD{h}")
            else:
                e.wait("scalar", f"out{(i - 3) % 3}", f"outw{h}_{i-3}")
            e.op("scalar", lambda E, i=i, r0=r0: E.dma_start(
                x1s[i % 3][:, :], x[r0 + i * 128:r0 + (i + 1) * 128, :]),
                f"ld{i % 3}", f"ldD2{h}_{i}", k=16)
            if i >= 1:
                qdq(i - 1)
            e.wait("scalar", f"ld{i % 3}", f"ldD2{h}_{i}")
            e.op("scalar", lambda E, i=i, ti=ti: E.activation(
                x1s[i % 3][:, :], x1s[i % 3][:, :], AF.Identity,
                scale=f(ti, 9), bias=f(ti, 10)), "act", f"xbD{h}_{i}")
            e.wait("vector", "act", f"xbD{h}_{i}")
            e.wait("vector", f"ot{i}", f"o1t{h}_{i}")
            e.op("vector", lambda E, i=i: E.tensor_tensor(
                x1s[i % 3][:, :], x1s[i % 3][:, :], o1t[:, i, :], ALU.add),
                "dve", f"x1b{h}_{i}")
            e.op("vector", lambda E: E.drain())
            if i >= 2:
                emit_out(i - 2)
        qdq(NT - 1)
        emit_out(NT - 2)
        emit_out(NT - 1)

    e.phase("F")
    e.wait("sync", "out0", f"outw{HALVES-1}_{6}")
    e.wait("sync", "out1", f"outw{HALVES-1}_{7}")
    e.wait("sync", "out2", f"outw{HALVES-1}_{5}")


def compute_psum_free(marks):
    freed = {}
    order = []
    for h in range(HALVES):
        for p in range(NP):
            order.append(((h, "B", p), p % 2, ("dve", f"zgB{h}_{p}")))
        for d_ in range(ND):
            order.append(((h, "C", d_), d_ % 2, ("dve", f"o1eC{h}_{d_}")))
        for i in range(NT):
            order.append(((h, "D", i), i % 2, ("dve", f"outD{h}_{i}")))
    last = {}
    for key, par, freemark in order:
        if par in last:
            sem, mk = last[par]
            freed[key] = (sem, marks.get(mk, 0))
        last[par] = freemark
    return freed


def build_program():
    nc = bass.Bass("TRN2", target_bir_lowering=False)
    t = {}
    t["x"] = nc.dram_tensor("x", [TCORE, D], F32, kind="ExternalInput")
    t["grep"] = nc.dram_tensor("grep", [128, D], F32, kind="ExternalInput")
    t["brep"] = nc.dram_tensor("brep", [128, D], F32, kind="ExternalInput")
    t["win"] = nc.dram_tensor("win", [2 * E, D], F32, kind="ExternalInput")
    t["wout"] = nc.dram_tensor("wout", [D, E], F32, kind="ExternalInput")
    t["w8"] = nc.dram_tensor("w8", [D, D], F8, kind="ExternalInput")
    t["dtrep"] = nc.dram_tensor("dtrep", [128, E], F32, kind="ExternalInput")
    t["dtcol"] = nc.dram_tensor("dtcol", [128, 32], F32, kind="ExternalInput")
    t["scol"] = nc.dram_tensor("scol", [128, 1], BF16, kind="ExternalInput")
    t["wscol"] = nc.dram_tensor("wscol", [128, 1], F32, kind="ExternalInput")
    t["out"] = nc.dram_tensor("out", [TCORE, D], F32, kind="ExternalOutput")
    t["winb"] = nc.dram_tensor("winb", [2 * E, D], BF16, kind="Internal")
    t["woutb"] = nc.dram_tensor("woutb", [D, E], BF16, kind="Internal")
    t["w3b"] = nc.dram_tensor("w3b", [D, D], BF16, kind="Internal")
    t["xnb"] = nc.dram_tensor("xnb", [HT, D], BF16, kind="Internal")
    t["o1b"] = nc.dram_tensor("o1b", [D, HT], BF16, kind="Internal")
    t["yhb"] = nc.dram_tensor("yhb", [HT, D], BF16, kind="Internal")

    with ExitStack() as ctx:
        en = ctx.enter_context
        t["st"] = en(nc.sbuf_tensor("st", [128, 16, 20], F32))
        t["dtw"] = en(nc.sbuf_tensor("dtw", [128, 32], F32))
        t["dtc"] = en(nc.sbuf_tensor("dtc", [128, 32], F32))
        t["cst"] = en(nc.sbuf_tensor("cst", [128, 4], F32))
        t["sm"] = en(nc.sbuf_tensor("sm", [128, 4], F32))
        t["scb"] = en(nc.sbuf_tensor("scb", [128, 1], BF16))
        t["gbf"] = en(nc.sbuf_tensor("gbf", [128, D], BF16))
        t["bbf"] = en(nc.sbuf_tensor("bbf", [128, D], BF16))
        t["XnT"] = en(nc.sbuf_tensor("XnT", [128, 16, HT], BF16))
        t["ATa"] = en(nc.sbuf_tensor("ATa", [128, 32768], BF16))
        t["WAa"] = en(nc.sbuf_tensor("WAa", [128, 32768], BF16))
        t["F32A"] = en(nc.sbuf_tensor("F32A", [128, 3, D], F32))
        t["ybuf"] = en(nc.sbuf_tensor("ybuf", [128, D], F32))
        t["SCR"] = en(nc.sbuf_tensor("SCR", [128, 6144], U8))
        t["P0"] = en(nc.psum_tensor("P0", [128, D], F32))
        t["P1"] = en(nc.psum_tensor("P1", [128, D], F32))

        sem_names = ["stp", "cast", "wzA", "wzB", "gsA", "gsB", "woA", "woB",
                     "w3t", "xnw", "xnt",
                     "o1wA", "o1wB", "ot0", "ot1", "ot2", "ot3", "ot4", "ot5",
                     "ot6", "ot7", "yhwA", "yhwB", "yhtA", "yhtB",
                     "ld0", "ld1", "ld2", "out0", "out1", "out2",
                     "pe", "dve", "act", "gp"]
        semh = {n: en(nc.semaphore(n)) for n in sem_names}

        t["psum_free"] = {}
        e1 = Em()
        emit(e1, t)
        t["psum_free"] = compute_psum_free(e1.marks)
        e1b = Em()
        emit(e1b, t)
        assert e1b.marks == e1.marks
        e2 = Em(marks=e1b.marks, marksem=e1b.marksem)
        emit(e2, t)
        assert e2.cnt == e1b.cnt

        for name, streams in e2.phases:
            with nc.Block() as block:
                for eng_name in ENGINES:
                    items = streams[eng_name]
                    if not items:
                        continue

                    def body(E, items=items):
                        for it in items:
                            if it[0] == "op":
                                _, fn, sem, k = it
                                ins = fn(E)
                                if sem is not None:
                                    ins.then_inc(semh[sem], k)
                            else:
                                _, sem, v = it
                                E.wait_ge(semh[sem], v)

                    getattr(block, eng_name)(body)

    return nc


def kernel(L0_attn_x, L0_attn_gamma, L0_attn_beta, L0_attn_W_in, L0_attn_W_out,
           L0_attn_dt, L0_ffn_weight, L0_ffn_weight_scale, L0_ffn_input_scale):
    X = np.ascontiguousarray(np.asarray(L0_attn_x).reshape(16384, D))
    gamma = np.ascontiguousarray(
        np.broadcast_to(np.asarray(L0_attn_gamma, np.float32).reshape(1, D), (128, D)))
    beta = np.ascontiguousarray(
        np.broadcast_to(np.asarray(L0_attn_beta, np.float32).reshape(1, D), (128, D)))
    win = np.ascontiguousarray(np.asarray(L0_attn_W_in, np.float32))
    wout = np.ascontiguousarray(np.asarray(L0_attn_W_out, np.float32))
    dtv = np.asarray(L0_attn_dt, np.float32).reshape(E)
    dtrep = np.ascontiguousarray(np.broadcast_to(dtv.reshape(1, E), (128, E)))
    dtcol = np.ascontiguousarray(dtv.reshape(32, 128).T)
    w8 = np.asarray(L0_ffn_weight).view(ml_dtypes.float8_e4m3)
    scol = np.ascontiguousarray(
        np.broadcast_to(np.asarray(L0_ffn_input_scale).reshape(1, 1), (128, 1)))
    wscol = np.ascontiguousarray(
        np.broadcast_to(np.asarray(L0_ffn_weight_scale, np.float32).reshape(1, 1),
                        (128, 1)))

    nc = build_program()
    in_maps = []
    for c in range(NCORE):
        in_maps.append(dict(
            x=np.ascontiguousarray(X[c * TCORE:(c + 1) * TCORE]),
            grep=gamma, brep=beta, win=win, wout=wout, w8=w8,
            dtrep=dtrep, dtcol=dtcol, scol=scol, wscol=wscol))
    res = run_bass_kernel_spmd(nc, in_maps, core_ids=list(range(NCORE)))
    full = np.concatenate([np.asarray(r["out"]) for r in res.results], axis=0)
    return full.reshape(4, 4096, D).astype(np.float32, copy=False)


# version 29
# speedup vs baseline: 1.0189x; 1.0189x over previous
import numpy as np
import ml_dtypes
import concourse.bass as bass
import concourse.mybir as mybir
from concourse.bass_utils import run_bass_kernel_spmd
from contextlib import ExitStack

F32 = mybir.dt.float32
BF16 = mybir.dt.bfloat16
F8 = mybir.dt.float8e4
U8 = mybir.dt.uint8
AF = mybir.ActivationFunctionType
ALU = mybir.AluOpType
AX = mybir.AxisListType

NCORE = 8
TCORE = 2048
HT = 1024
NT = 8
D = 2048
E = 4096
NP = 32
NG = 8
ND = 16
NDC = 4
EPS = 1e-5
RD = 1.0 / 2048.0

ENGINES = ("sync", "scalar", "vector", "gpsimd", "tensor")
HALVES = 2


class Em:

    def __init__(self, marks=None, marksem=None):
        self.final = marks is not None
        self.marks = {} if marks is None else marks
        self.marksem = {} if marksem is None else marksem
        self.cnt = {}
        self.phases = []
        self.cur = None

    def phase(self, name):
        self.cur = {e: [] for e in ENGINES}
        self.phases.append((name, self.cur))

    def op(self, eng, fn, sem=None, mark=None, k=1):
        n = None
        if sem is not None:
            self.cnt[sem] = self.cnt.get(sem, 0) + k
            n = self.cnt[sem]
            if mark is not None:
                if not self.final:
                    self.marks[mark] = n
                    self.marksem[mark] = sem
        if self.final:
            self.cur[eng].append(("op", fn, sem, k))

    def wait(self, eng, sem, mark=None, value=None):
        if self.final:
            if value is None:
                assert mark in self.marks, f"undefined mark {mark}"
                assert self.marksem[mark] == sem, (
                    f"mark {mark} is on sem {self.marksem[mark]}, waited on {sem}")
            v = value if value is not None else self.marks[mark]
            if v > 0:
                self.cur[eng].append(("wait", sem, v))

    def at(self, mark):
        return self.marks.get(mark, 0)


def emit(e, t):

    XnT = t["XnT"]
    ATa = t["ATa"]
    WAa = t["WAa"]
    F32A = t["F32A"]
    ybuf = t["ybuf"]
    o1e = ybuf.bitcast(BF16)
    SCR = t["SCR"]
    SCRb = SCR.bitcast(BF16)
    SCR8 = SCR.bitcast(F8)
    gbf = t["gbf"]; bbf = t["bbf"]
    dtw = t["dtw"]
    dtc = t["dtc"]
    cst = t["cst"]
    sm = t["sm"]
    scb = t["scb"]
    st = t["st"]
    P = (t["P0"], t["P1"])

    x = t["x"]; grep = t["grep"]; brep = t["brep"]
    win = t["win"]; wout = t["wout"]; w8 = t["w8"]
    dtrep = t["dtrep"]; out = t["out"]
    winb = t["winb"]; woutb = t["woutb"]; w3b = t["w3b"]
    xnb = t["xnb"]; o1b = t["o1b"]; yhb = t["yhb"]
    wscol = t["wscol"]

    def zs(par):
        return WAa[:, par * 8192:(par + 1) * 8192].rearrange("p (k e) -> p k e", k=16)

    def gs(par):
        return WAa[:, 16384 + par * 8192:16384 + (par + 1) * 8192].rearrange(
            "p (k e) -> p k e", k=16)

    def wo(par):
        return WAa[:, par * 16384:(par + 1) * 16384].rearrange("p (k e) -> p k e", k=32)

    w3T = WAa[:, 0:32768].rearrange("p (k o) -> p k o", k=16)

    def ATl(et):
        return ATa[:, et * 1024:(et + 1) * 1024]

    yhT = ATa[:, 0:8192].rearrange("p (u k c) -> p u k c", u=2, k=16)
    o1t = ATa[:, 8192:24576].rearrange("p (u c) -> p u c", u=8)
    lob = ATa[:, 24576:26624]
    hib = ATa[:, 26624:28672]
    yh = ATa[:, 28672:30720]
    xnt_stage = ATa[:, 30720:32768]

    zcb = SCRb[:, 0:1024]
    gcb = SCRb[:, 1024:2048]
    zgb = SCRb[:, 2048:3072]
    qlo = SCR8[:, 0:2048]
    qhi = SCR8[:, 2048:4096]
    msk = SCR[:, 4096:6144]

    xs = (F32A[:, 0, :], F32A[:, 1, :])
    txg = F32A[:, 2, :]
    x1s = (F32A[:, 0, :], F32A[:, 1, :], F32A[:, 2, :])
    dtrep_s = XnT.bitcast(F32).rearrange(
        "p a b -> p (a b)")[:, 0:4096]

    def f(ti, j):
        return st[:, ti, j:j + 1]

    e.phase("S")
    e.op("scalar", lambda E: E.dma_start(dtc[:, :], t["dtcol"][:, :]), "stp", k=16)
    e.op("scalar", lambda E: E.dma_start(scb[:, :], t["scol"][:, :]), "stp", k=16)
    e.op("scalar", lambda E: E.dma_start(cst[:, 3:4], wscol[:, :]), "stp", "ld_all", k=16)
    e.wait("vector", "stp", "ld_all")
    e.op("vector", lambda E: E.tensor_copy(cst[:, 2:3], scb[:, :]))
    e.op("vector", lambda E: E.drain())
    e.op("vector", lambda E: E.reciprocal(cst[:, 0:1], cst[:, 2:3]))
    e.op("vector", lambda E: E.drain())
    e.op("vector", lambda E: E.tensor_tensor(cst[:, 1:2], cst[:, 2:3], cst[:, 3:4], ALU.mult),
         "dve", "consts")
    e.op("gpsimd", lambda E: E.dma_start(dtrep_s[:, :], dtrep[:, :]), "cast", "ld_dt", k=16)
    e.wait("vector", "cast", "ld_dt")
    e.op("vector", lambda E: E.tensor_scalar(dtrep_s[:, :], dtrep_s[:, :], -1.0, None, ALU.mult))
    e.op("vector", lambda E: E.drain())
    e.op("vector", lambda E: E.reduce_max(sm[:, 0:1], dtrep_s[:, :], axis=AX.X))
    e.op("vector", lambda E: E.drain())
    e.op("vector", lambda E: E.tensor_scalar(sm[:, 1:2], sm[:, 0:1], -1.0, None, ALU.mult),
         "dve", "negm")
    e.wait("scalar", "dve", "negm")
    e.op("scalar", lambda E: E.activation(dtrep_s[:, :], dtrep_s[:, :], AF.Exp,
                                          bias=sm[:, 1:2]), "act", "expden")
    e.wait("vector", "act", "expden")
    e.op("vector", lambda E: E.reduce_sum(sm[:, 2:3], dtrep_s[:, :], axis=AX.X))
    e.op("vector", lambda E: E.drain())
    e.op("vector", lambda E: E.reciprocal(sm[:, 3:4], sm[:, 2:3]), "dve", "rs")
    e.wait("scalar", "stp", "ld_all")
    e.op("scalar", lambda E: E.activation(dtw[:, :], dtc[:, :], AF.Exp,
                                          scale=-1.0, bias=sm[:, 1:2]), "act", "expcol")
    e.wait("vector", "act", "expcol")
    e.op("vector", lambda E: E.drain())
    e.op("vector", lambda E: E.tensor_scalar(dtw[:, :], dtw[:, :], sm[:, 3:4], None,
                                             ALU.mult), "dve", "dtw")
    e.op("gpsimd", lambda E: E.dma_start(gbf[:, :], grep[:, :]), "cast", "cgb", k=16)
    e.op("gpsimd", lambda E: E.dma_start(bbf[:, :], brep[:, :]), "cast", "cbb", k=16)
    def emit_cast_zg(g):
        e.op("gpsimd", lambda E, g=g: E.dma_start(
            winb[512 * g:512 * (g + 1), :], win[512 * g:512 * (g + 1), :]),
            "cast", f"cz{g}", k=16)
        e.op("gpsimd", lambda E, g=g: E.dma_start(
            winb[4096 + 512 * g:4096 + 512 * (g + 1), :],
            win[4096 + 512 * g:4096 + 512 * (g + 1), :]), "cast", f"cg{g}", k=16)

    cast_steps = []
    for g in range(1, NG):
        cast_steps.append(("z", g))
        cast_steps.append(("g", g))
    for c in range(NDC):
        cast_steps.append(("w", c))
    for c in range(2):
        cast_steps.append(("3", c))

    def emit_cast_step(step):
        kind, idx = step
        if kind == "z":
            e.op("gpsimd", lambda E, g=idx: E.dma_start(
                winb[512 * g:512 * (g + 1), :], win[512 * g:512 * (g + 1), :]),
                "cast", f"cz{idx}", k=16)
        elif kind == "g":
            e.op("gpsimd", lambda E, g=idx: E.dma_start(
                winb[4096 + 512 * g:4096 + 512 * (g + 1), :],
                win[4096 + 512 * g:4096 + 512 * (g + 1), :]),
                "cast", f"cg{idx}", k=16)
        elif kind == "w":
            e.op("gpsimd", lambda E, c=idx: E.dma_start(
                woutb[512 * c:512 * (c + 1), :], wout[512 * c:512 * (c + 1), :]),
                "cast", f"cw{idx}", k=16)
        else:
            e.op("gpsimd", lambda E, c=idx: E.dma_start(
                w3b[1024 * c:1024 * (c + 1), :], w8[1024 * c:1024 * (c + 1), :]),
                "cast", f"c3{idx}", k=16)

    emit_cast_zg(0)

    for h in range(HALVES):
        hb = h * NT
        r0 = h * HT

        e.phase(f"A{h}")
        for i in range(NT):
            ti = hb + i
            if h == 1 and i < 2:
                e.wait("scalar", "out0", value=e.at(f"outw{0}_{6}"))
                e.wait("scalar", "out1", value=e.at(f"outw{0}_{7}"))
                e.wait("scalar", "out2", value=e.at(f"outw{0}_{5}"))
            if i >= 2:
                e.wait("scalar", "act", f"hAa{h}_{i-2}")
                e.wait("scalar", "dve", f"hAd{h}_{i-2}")
            e.op("scalar", lambda E, i=i, r0=r0: E.dma_start(
                xs[i % 2][:, :], x[r0 + i * 128:r0 + (i + 1) * 128, :]),
                f"ld{i % 2}", f"ldxA{h}_{i}", k=16)
            e.wait("scalar", f"ld{i % 2}", f"ldxA{h}_{i}")
            e.op("scalar", lambda E, i=i, ti=ti: E.activation(
                txg[:, :], xs[i % 2][:, :], AF.Square, accum_out=f(ti, 2)),
                "act", f"hAa{h}_{i}")
            e.wait("vector", f"ld{i % 2}", f"ldxA{h}_{i}")
            e.op("vector", lambda E, i=i, ti=ti: E.reduce_sum(
                f(ti, 0), xs[i % 2][:, :], axis=AX.X), "dve", f"hAd{h}_{i}")
            e.op("vector", lambda E: E.drain())

        def bv(fn, sem=None, mark=None, hb=hb):
            for q in range(NT):
                last = q == NT - 1
                e.op("vector", (lambda E, q=q, hb=hb: fn(E, hb + q)),
                     sem if last else None, mark if last else None)
            e.op("vector", lambda E: E.drain())

        def ba(fn, sem=None, mark=None, hb=hb):
            for q in range(NT):
                last = q == NT - 1
                e.op("scalar", (lambda E, q=q, hb=hb: fn(E, hb + q)),
                     sem if last else None, mark if last else None)

        e.wait("vector", "act", f"hAa{h}_{NT-1}")
        bv(lambda E, ti: E.tensor_scalar(f(ti, 1), f(ti, 0), RD, None, ALU.mult))
        bv(lambda E, ti: E.tensor_scalar(f(ti, 3), f(ti, 2), RD, None, ALU.mult))
        bv(lambda E, ti: E.tensor_tensor(f(ti, 4), f(ti, 1), f(ti, 1), ALU.mult))
        bv(lambda E, ti: E.tensor_tensor(f(ti, 5), f(ti, 3), f(ti, 4), ALU.subtract))
        bv(lambda E, ti: E.tensor_scalar(f(ti, 6), f(ti, 5), EPS, None, ALU.add),
           "dve", f"bA1{h}")
        e.wait("scalar", "dve", f"bA1{h}")
        ba(lambda E, ti: E.activation(f(ti, 7), f(ti, 6), AF.Sqrt), "act", f"bA1s{h}")
        e.wait("vector", "act", f"bA1s{h}")
        bv(lambda E, ti: E.reciprocal(f(ti, 8), f(ti, 7)))
        bv(lambda E, ti: E.tensor_tensor(f(ti, 4), f(ti, 8), f(ti, 8), ALU.mult))
        bv(lambda E, ti: E.tensor_tensor(f(ti, 4), f(ti, 4), f(ti, 6), ALU.mult))
        bv(lambda E, ti: E.tensor_scalar(f(ti, 4), f(ti, 4), -0.5, 1.5, ALU.mult, ALU.add))
        bv(lambda E, ti: E.tensor_tensor(f(ti, 8), f(ti, 8), f(ti, 4), ALU.mult))
        bv(lambda E, ti: E.tensor_scalar(f(ti, 9), f(ti, 8), 1.0, None, ALU.add))
        bv(lambda E, ti: E.scalar_tensor_tensor(
            f(ti, 10), f(ti, 1), -1.0, f(ti, 8), ALU.mult, ALU.mult))
        bv(lambda E, ti: E.tensor_tensor(f(ti, 11), f(ti, 8), f(ti, 8), ALU.mult))
        bv(lambda E, ti: E.tensor_tensor(f(ti, 11), f(ti, 11), f(ti, 5), ALU.mult))
        bv(lambda E, ti: E.tensor_scalar(f(ti, 11), f(ti, 11), EPS, None, ALU.add),
           "dve", f"bA2{h}")
        e.wait("scalar", "dve", f"bA2{h}")
        ba(lambda E, ti: E.activation(f(ti, 12), f(ti, 11), AF.Sqrt), "act", f"bA2s{h}")
        e.wait("vector", "act", f"bA2s{h}")
        bv(lambda E, ti: E.reciprocal(f(ti, 13), f(ti, 12)))
        bv(lambda E, ti: E.tensor_tensor(f(ti, 4), f(ti, 13), f(ti, 13), ALU.mult))
        bv(lambda E, ti: E.tensor_tensor(f(ti, 4), f(ti, 4), f(ti, 11), ALU.mult))
        bv(lambda E, ti: E.tensor_scalar(f(ti, 4), f(ti, 4), -0.5, 1.5, ALU.mult, ALU.add))
        bv(lambda E, ti: E.tensor_tensor(f(ti, 13), f(ti, 13), f(ti, 4), ALU.mult))
        bv(lambda E, ti: E.tensor_tensor(f(ti, 14), f(ti, 8), f(ti, 13), ALU.mult))
        bv(lambda E, ti: E.tensor_tensor(f(ti, 15), f(ti, 10), f(ti, 13), ALU.mult),
           "dve", f"bA{h}")

        tg = (txg, ybuf)
        xst = (xnt_stage, yh)
        for i in range(NT):
            ti = hb + i
            if i < 2:
                e.wait("scalar", "dve", f"bA{h}")
            else:
                e.wait("scalar", "act", f"tA{h}_{i-2}")
            e.op("scalar", lambda E, i=i, r0=r0: E.dma_start(
                xs[i % 2][:, :], x[r0 + i * 128:r0 + (i + 1) * 128, :]),
                f"ld{i % 2}", f"ldA2{h}_{i}", k=16)
            e.wait("scalar", f"ld{i % 2}", f"ldA2{h}_{i}")
            if i >= 2:
                e.wait("scalar", "gp", f"xnA{h}_{i-2}")
            e.op("scalar", lambda E, i=i, ti=ti: E.activation(
                tg[i % 2][:, :], xs[i % 2][:, :], AF.Identity,
                scale=f(ti, 14), bias=f(ti, 15)), "act", f"tA{h}_{i}")
            e.wait("vector", "act", f"tA{h}_{i}")
            if h == 0 and i == 0:
                e.wait("vector", "cast", "cgb")
            e.op("vector", lambda E, i=i: E.tensor_tensor(
                tg[i % 2][:, :], tg[i % 2][:, :], gbf[:, :], ALU.mult),
                "dve", f"xgA{h}_{i}")
            e.wait("gpsimd", "dve", f"xgA{h}_{i}")
            if h == 0 and i == 0:
                e.wait("gpsimd", "cast", "cbb")
            if i >= 2:
                e.wait("gpsimd", "xnw", f"xnw{h}_{i-2}")
            if h == 1 and i == 0:
                e.wait("gpsimd", "pe", f"peC{0}_{15}")
            e.op("gpsimd", lambda E, i=i: E.tensor_tensor(
                xst[i % 2][:, :], tg[i % 2][:, :], bbf[:, :], ALU.add),
                "gp", f"xnA{h}_{i}")
            e.wait("sync", "gp", f"xnA{h}_{i}")
            e.op("sync", lambda E, i=i: E.dma_start(
                xnb[i * 128:(i + 1) * 128, :], xst[i % 2][:, :]),
                "xnw", f"xnw{h}_{i}", k=16)
        e.wait("scalar", "xnw", f"xnw{h}_{7}")
        if h == 0:
            e.wait("scalar", "dve", "rs")
        if h == 1:
            e.wait("scalar", "pe", f"peB{0}_{NP-1}")
        for dp in range(16):
            e.op("scalar", lambda E, dp=dp: E.dma_start(
                XnT[:, dp, :], xnb[0:HT, dp * 128:(dp + 1) * 128], transpose=True),
                "xnt", f"xnt{h}_{dp}", k=16)

        e.phase(f"B{h}")
        for g in range(NG):
            par = g % 2
            if h == 0:
                e.wait("sync", "cast", f"cz{g}")
                e.wait("sync", "cast", f"cg{g}")
            if h == 1 and g < 2:
                e.wait("sync", "pe", f"peD{0}_{7}")
            if g >= 2:
                e.wait("sync", "pe", f"peB{h}_{4*(g-2)+3}")
            for dp in range(16):
                e.op("sync", lambda E, g=g, dp=dp, par=par: E.dma_start(
                    zs(par)[:, dp, :],
                    winb[512 * g:512 * (g + 1), dp * 128:(dp + 1) * 128],
                    transpose=True), "wzA" if par == 0 else "wzB", k=16)
            for dp in range(16):
                e.op("sync", lambda E, g=g, dp=dp, par=par: E.dma_start(
                    gs(par)[:, dp, :],
                    winb[4096 + 512 * g:4096 + 512 * (g + 1), dp * 128:(dp + 1) * 128],
                    transpose=True), "wzA" if par == 0 else "wzB",
                    f"wz{h}_{g}" if dp == 15 else None, k=16)

        for p in range(NP):
            g = p // 4
            par = g % 2
            sl = (p % 4) * 128
            if p % 4 == 0:
                e.wait("tensor", "wzA" if par == 0 else "wzB", f"wz{h}_{g}")
            if p == 0:
                e.wait("tensor", "xnt", f"xnt{h}_{15}")
                e.wait("tensor", "dve", "dtw")
            pf = t["psum_free"].get((h, "B", p))
            if pf:
                e.wait("tensor", pf[0], value=pf[1])
            for k in range(16):
                for tc in range(2):
                    e.op("tensor", lambda E, p=p, k=k, tc=tc, par=par, sl=sl: E.matmul(
                        P[p % 2][:, tc * 512:(tc + 1) * 512],
                        zs(par)[:, k, sl:sl + 128],
                        XnT[:, k, tc * 512:(tc + 1) * 512],
                        start=(k == 0), stop=(k == 15)))
            for k in range(16):
                for tc in range(2):
                    e.op("tensor", lambda E, p=p, k=k, tc=tc, par=par, sl=sl: E.matmul(
                        P[p % 2][:, 1024 + tc * 512:1024 + (tc + 1) * 512],
                        gs(par)[:, k, sl:sl + 128],
                        XnT[:, k, tc * 512:(tc + 1) * 512],
                        start=(k == 0), stop=(k == 15)),
                        sem="pe" if (k == 15 and tc == 1) else None,
                        mark=f"peB{h}_{p}" if (k == 15 and tc == 1) else None)

            e.wait("vector", "pe", f"peB{h}_{p}")
            if p >= 1:
                e.wait("vector", "gp", f"mulB{h}_{p-1}")
            e.op("vector", lambda E, p=p: E.tensor_scalar(
                zcb[:, :], P[p % 2][:, 0:1024], -15.0, 15.0, ALU.max, ALU.min),
                "dve", f"czB{h}_{p}")
            e.op("vector", lambda E, p=p: E.tensor_scalar(
                gcb[:, :], P[p % 2][:, 1024:2048], -15.0, 15.0, ALU.max, ALU.min),
                "dve", f"cgB{h}_{p}")
            e.wait("vector", "act", f"sigB{h}_{p}")
            e.op("vector", lambda E, p=p: E.tensor_tensor(
                zgb[:, :], P[p % 2][:, 0:1024], zcb[:, :], ALU.mult),
                "dve", f"zgB{h}_{p}")
            e.op("vector", lambda E: E.drain())
            e.wait("scalar", "dve", f"czB{h}_{p}")
            e.op("scalar", lambda E: E.activation(zcb[:, :], zcb[:, :], AF.Sigmoid),
                 "act", f"sigB{h}_{p}")
            e.wait("scalar", "dve", f"cgB{h}_{p}")
            e.wait("scalar", "dve", f"zgB{h}_{p}")
            e.op("scalar", lambda E: E.activation(zcb[:, :], gcb[:, :], AF.Sigmoid),
                 "act", f"sigG{h}_{p}")
            e.wait("gpsimd", "act", f"sigG{h}_{p}")
            if h == 1 and p == 0:
                e.wait("gpsimd", "dve", f"selD{0}_{7}")
            e.op("gpsimd", lambda E: E.tensor_tensor(
                gcb[:, :], gcb[:, :], zcb[:, :], ALU.mult), "gp", f"mulB{h}_{p}")
            if h == 0 and p < len(cast_steps):
                emit_cast_step(cast_steps[p])
            e.wait("vector", "gp", f"mulB{h}_{p}")
            if h == 1 and p == 0:
                e.wait("vector", "xnw", f"xnw{1}_{7}")
                e.wait("vector", "pe", f"peD{0}_{7}")
                e.wait("vector", "yhwB", f"yhw{0}_{7}")
            if h == 0 and p == 0:
                e.wait("vector", "dve", "dtw")
            e.op("vector", lambda E, p=p: E.scalar_tensor_tensor(
                ATl(p)[:, :], zgb[:, :], dtw[:, p:p + 1], gcb[:, :],
                ALU.mult, ALU.mult), "dve", f"sttB{h}_{p}")

        e.phase(f"C{h}")
        def emit_wo(c):
            par = c % 2
            q = "sync" if par == 0 else "scalar"
            if h == 0:
                e.wait(q, "cast", f"cw{c}")
            if c < 2:
                e.wait(q, "pe", f"peB{h}_{27 if c == 0 else 31}")
            else:
                e.wait(q, "pe", f"peC{h}_{4*(c-2)+3}")
            for ep in range(32):
                e.op(q, lambda E, c=c, ep=ep, par=par: E.dma_start(
                    wo(par)[:, ep, :],
                    woutb[512 * c:512 * (c + 1), ep * 128:(ep + 1) * 128],
                    transpose=True), "woA" if par == 0 else "woB",
                    f"wo{h}_{c}" if ep == 31 else None, k=16)

        emit_wo(0)
        emit_wo(1)
        emit_wo(2)
        for dt_ in range(ND):
            if dt_ == 8:
                emit_wo(3)
            c = dt_ // 4
            par = c % 2
            sl = (dt_ % 4) * 128
            if dt_ % 4 == 0:
                e.wait("tensor", "woA" if par == 0 else "woB", f"wo{h}_{c}")
            if dt_ == 0:
                e.wait("tensor", "dve", f"sttB{h}_{NP-1}")
            pf = t["psum_free"].get((h, "C", dt_))
            if pf:
                e.wait("tensor", pf[0], value=pf[1])
            for ep in range(32):
                for tc in range(2):
                    e.op("tensor", lambda E, dt_=dt_, ep=ep, tc=tc, par=par, sl=sl: E.matmul(
                        P[dt_ % 2][:, tc * 512:(tc + 1) * 512],
                        wo(par)[:, ep, sl:sl + 128],
                        ATl(ep)[:, tc * 512:(tc + 1) * 512],
                        start=(ep == 0), stop=(ep == 31)),
                        sem="pe" if (ep == 31 and tc == 1) else None,
                        mark=f"peC{h}_{dt_}" if (ep == 31 and tc == 1) else None)
            e.wait("vector", "pe", f"peC{h}_{dt_}")
            if dt_ >= 2:
                e.wait("vector", "o1wA" if dt_ % 2 == 0 else "o1wB", f"o1w{h}_{dt_-2}")
            e.op("vector", lambda E, dt_=dt_: E.tensor_copy(
                o1e[:, (dt_ % 2) * 1024:(dt_ % 2 + 1) * 1024], P[dt_ % 2][:, 0:1024]),
                "dve", f"o1eC{h}_{dt_}")
            e.wait("scalar", "dve", f"o1eC{h}_{dt_}")
            e.op("scalar", lambda E, dt_=dt_: E.dma_start(
                o1b[dt_ * 128:(dt_ + 1) * 128, :],
                o1e[:, (dt_ % 2) * 1024:(dt_ % 2 + 1) * 1024]),
                "o1wA" if dt_ % 2 == 0 else "o1wB", f"o1w{h}_{dt_}", k=16)

        e.phase(f"D{h}")
        e.wait("sync", "pe", f"peC{h}_{15}")
        e.wait("sync", "o1wA", f"o1w{h}_{14}")
        e.wait("sync", "o1wB", f"o1w{h}_{15}")
        for i in range(NT):
            e.op("sync", lambda E, i=i: E.dma_start(
                o1t[:, i, :], o1b[0:2048, i * 128:(i + 1) * 128], transpose=True),
                f"ot{i}", f"o1t{h}_{i}", k=16)
        if h == 0:
            e.wait("sync", "cast", "c31")
        for k in range(16):
            e.op("sync", lambda E, k=k: E.dma_start(
                w3T[:, k, :], w3b[0:2048, k * 128:(k + 1) * 128], transpose=True),
                "w3t", f"w3t{h}_{k}" if k == 15 else None, k=16)

        for i in range(NT):
            ti = hb + i
            if i == 0:
                e.wait("scalar", "act", f"tA{h}_{NT-2}")
            elif i == 1:
                e.wait("scalar", "act", f"tA{h}_{NT-1}")
            else:
                e.wait("scalar", "dve", f"hDd{h}_{i-2}")
                e.wait("scalar", "act", f"hDa{h}_{i-2}")
            e.op("scalar", lambda E, i=i, r0=r0: E.dma_start(
                xs[i % 2][:, :], x[r0 + i * 128:r0 + (i + 1) * 128, :]),
                f"ld{i % 2}", f"ldxD{h}_{i}", k=16)
            e.wait("scalar", f"ld{i % 2}", f"ldxD{h}_{i}")
            e.op("scalar", lambda E, i=i, ti=ti: E.activation(
                xs[i % 2][:, :], xs[i % 2][:, :], AF.Identity,
                scale=f(ti, 9), bias=f(ti, 10)), "act", f"xaD{h}_{i}")
            if i >= 1:
                e.wait("scalar", "dve", f"x1D{h}_{i-1}")
                e.op("scalar", lambda E, i=i, ti=ti: E.activation(
                    ybuf[:, :], xs[(i - 1) % 2][:, :], AF.Square,
                    accum_out=f(ti - 1, 16)), "act", f"hDa{h}_{i-1}")
            e.wait("vector", "act", f"xaD{h}_{i}")
            e.wait("vector", f"ot{i}", f"o1t{h}_{i}")
            e.op("vector", lambda E, i=i: E.tensor_tensor(
                xs[i % 2][:, :], xs[i % 2][:, :], o1t[:, i, :], ALU.add),
                "dve", f"x1D{h}_{i}")
            e.op("vector", lambda E: E.drain())
            e.op("vector", lambda E, i=i, ti=ti: E.reduce_sum(
                f(ti, 17), xs[i % 2][:, :], axis=AX.X), "dve", f"hDd{h}_{i}")
            e.op("vector", lambda E: E.drain())
        e.wait("scalar", "dve", f"x1D{h}_{NT-1}")
        e.op("scalar", lambda E, ti=hb + NT - 1: E.activation(
            ybuf[:, :], xs[(NT - 1) % 2][:, :], AF.Square,
            accum_out=f(ti, 16)), "act", f"hDa{h}_{NT-1}")

        def bv(fn, sem=None, mark=None, hb=hb):
            for q in range(NT):
                last = q == NT - 1
                e.op("vector", (lambda E, q=q, hb=hb: fn(E, hb + q)),
                     sem if last else None, mark if last else None)
            e.op("vector", lambda E: E.drain())

        def ba(fn, sem=None, mark=None, hb=hb):
            for q in range(NT):
                last = q == NT - 1
                e.op("scalar", (lambda E, q=q, hb=hb: fn(E, hb + q)),
                     sem if last else None, mark if last else None)

        e.wait("vector", "act", f"hDa{h}_{NT-1}")
        if h == 0:
            e.wait("vector", "dve", "consts")
        bv(lambda E, ti: E.tensor_scalar(f(ti, 18), f(ti, 17), RD, None, ALU.mult))
        bv(lambda E, ti: E.tensor_scalar(f(ti, 19), f(ti, 16), RD, None, ALU.mult))
        bv(lambda E, ti: E.tensor_tensor(f(ti, 4), f(ti, 18), f(ti, 18), ALU.mult))
        bv(lambda E, ti: E.tensor_tensor(f(ti, 19), f(ti, 19), f(ti, 4), ALU.subtract))
        bv(lambda E, ti: E.tensor_scalar(f(ti, 19), f(ti, 19), EPS, None, ALU.add),
           "dve", f"bD1{h}")
        e.wait("scalar", "dve", f"bD1{h}")
        ba(lambda E, ti: E.activation(f(ti, 5), f(ti, 19), AF.Sqrt), "act", f"bD1s{h}")
        e.wait("vector", "act", f"bD1s{h}")
        bv(lambda E, ti: E.reciprocal(f(ti, 6), f(ti, 5)))
        bv(lambda E, ti: E.tensor_tensor(f(ti, 4), f(ti, 6), f(ti, 6), ALU.mult))
        bv(lambda E, ti: E.tensor_tensor(f(ti, 4), f(ti, 4), f(ti, 19), ALU.mult))
        bv(lambda E, ti: E.tensor_scalar(f(ti, 4), f(ti, 4), -0.5, 1.5, ALU.mult, ALU.add))
        bv(lambda E, ti: E.tensor_tensor(f(ti, 6), f(ti, 6), f(ti, 4), ALU.mult))
        bv(lambda E, ti: E.tensor_scalar(f(ti, 7), f(ti, 6), cst[:, 0:1], None, ALU.mult))
        bv(lambda E, ti: E.scalar_tensor_tensor(
            f(ti, 8), f(ti, 18), -1.0, f(ti, 7), ALU.mult, ALU.mult), "dve", f"bD{h}")

        def emit_out(i):
            e.wait("vector", "pe", f"peD{h}_{i}")
            e.op("vector", lambda E, i=i: E.scalar_tensor_tensor(
                x1s[i % 3][:, :], P[i % 2][:, :], cst[:, 1:2], x1s[i % 3][:, :],
                ALU.mult, ALU.add), "dve", f"outD{h}_{i}")
            e.wait("scalar", "dve", f"outD{h}_{i}")
            e.op("scalar", lambda E, i=i, r0=r0: E.dma_start(
                out[r0 + i * 128:r0 + (i + 1) * 128, :], x1s[i % 3][:, :]),
                f"out{i % 3}", f"outw{h}_{i}", k=16)

        XnTb = XnT.rearrange("p a b -> p (a b)")
        XnT8 = XnT.bitcast(F8)
        XnTu = XnT.bitcast(U8)
        ybs = (ybuf, XnTb[:, 4 * 1024:8 * 1024].bitcast(F32))
        lobs = (lob, XnTb[:, 0:2048])
        hibs = (hib, XnTb[:, 2048:4096])
        yhs = (yh, XnTb[:, 12 * 1024:14 * 1024])
        msks = (msk, XnTu[:, 8, :])
        qlos = (qlo, XnT8[:, 9, :])
        qhis = (qhi, XnT8[:, 10, :])

        def qdq(j):
            q = j % 2
            u = j // 2
            e.wait("scalar", "dve", f"x1b{h}_{j}")
            if j >= 2:
                e.wait("scalar", "dve", f"upD{h}_{j-2}")
                e.wait("scalar", "dve", f"selD{h}_{j-2}")
            e.op("scalar", lambda E, j=j, ti=hb + j, q=q: E.activation(
                ybs[q][:, :], x1s[j % 3][:, :], AF.Identity,
                scale=f(ti, 7), bias=f(ti, 8)), "act", f"yD{h}_{j}")
            e.wait("vector", "act", f"yD{h}_{j}")
            e.op("vector", lambda E, q=q: E.tensor_scalar(
                ybs[q][:, :], ybs[q][:, :], -448.0, 448.0, ALU.max, ALU.min),
                "dve", f"vcD{h}_{j}")
            e.wait("scalar", "dve", f"vcD{h}_{j}")
            e.op("scalar", lambda E, q=q: E.activation(
                qlos[q][:, :], ybs[q][:, :], AF.Copy))
            e.op("scalar", lambda E, q=q: E.activation(
                qhis[q][:, :], ybs[q][:, :], AF.Copy, scale=0.5))
            e.op("scalar", lambda E: E.drain())
            e.op("scalar", lambda E, q=q: E.activation(
                msks[q][:, :], qlos[q][:, :], AF.Is_finite), "act", f"m8D{h}_{j}")
            e.wait("vector", "act", f"m8D{h}_{j}")
            e.op("vector", lambda E, q=q: E.tensor_copy(lobs[q][:, :], qlos[q][:, :]))
            e.op("vector", lambda E, q=q: E.tensor_scalar(
                hibs[q][:, :], qhis[q][:, :], 2.0, None, ALU.mult), "dve", f"upD{h}_{j}")
            e.op("vector", lambda E: E.drain())
            if j >= 2:
                e.wait("vector", "yhwA" if ((j - 2) // 2) % 2 == 0 else "yhwB",
                       f"yhw{h}_{j-2}")
            e.op("vector", lambda E, q=q: E.select(
                yhs[q][:, :], msks[q][:, :], lobs[q][:, :], hibs[q][:, :]),
                "dve", f"selD{h}_{j}")
            e.wait("scalar", "dve", f"selD{h}_{j}")
            e.op("scalar", lambda E, j=j, q=q: E.dma_start(
                yhb[j * 128:(j + 1) * 128, :], yhs[q][:, :]),
                "yhwA" if u % 2 == 0 else "yhwB", f"yhw{h}_{j}", k=16)
            if j % 2 == 1:
                e.wait("sync", "yhwA" if u % 2 == 0 else "yhwB", f"yhw{h}_{j}")
                if u >= 2:
                    e.wait("sync", "pe", f"peD{h}_{2*(u-2)+1}")
                for k in range(16):
                    e.op("sync", lambda E, u=u, k=k: E.dma_start(
                        yhT[:, u % 2, k, :],
                        yhb[(u * 256):(u * 256 + 256), k * 128:(k + 1) * 128],
                        transpose=True), "yhtA" if u % 2 == 0 else "yhtB",
                        f"yht{h}_{u}" if k == 15 else None, k=16)
            if j % 2 == 0:
                e.wait("tensor", "yhtA" if u % 2 == 0 else "yhtB", f"yht{h}_{u}")
            if j == 0:
                e.wait("tensor", "w3t", f"w3t{h}_{15}")
            pf = t["psum_free"].get((h, "D", j))
            if pf:
                e.wait("tensor", pf[0], value=pf[1])
            for k in range(16):
                for oc in range(4):
                    e.op("tensor", lambda E, j=j, u=u, k=k, oc=oc: E.matmul(
                        P[j % 2][:, oc * 512:(oc + 1) * 512],
                        yhT[:, u % 2, k, (j % 2) * 128:(j % 2) * 128 + 128],
                        w3T[:, k, oc * 512:(oc + 1) * 512],
                        start=(k == 0), stop=(k == 15)),
                        sem="pe" if (k == 15 and oc == 3) else None,
                        mark=f"peD{h}_{j}" if (k == 15 and oc == 3) else None)

        for i in range(NT):
            ti = hb + i
            if i < 3:
                e.wait("scalar", "dve", f"bD{h}")
            else:
                e.wait("scalar", f"out{(i - 3) % 3}", f"outw{h}_{i-3}")
            e.op("scalar", lambda E, i=i, r0=r0: E.dma_start(
                x1s[i % 3][:, :], x[r0 + i * 128:r0 + (i + 1) * 128, :]),
                f"ld{i % 3}", f"ldD2{h}_{i}", k=16)
            if i >= 1:
                qdq(i - 1)
            e.wait("scalar", f"ld{i % 3}", f"ldD2{h}_{i}")
            e.op("scalar", lambda E, i=i, ti=ti: E.activation(
                x1s[i % 3][:, :], x1s[i % 3][:, :], AF.Identity,
                scale=f(ti, 9), bias=f(ti, 10)), "act", f"xbD{h}_{i}")
            e.wait("vector", "act", f"xbD{h}_{i}")
            e.wait("vector", f"ot{i}", f"o1t{h}_{i}")
            e.op("vector", lambda E, i=i: E.tensor_tensor(
                x1s[i % 3][:, :], x1s[i % 3][:, :], o1t[:, i, :], ALU.add),
                "dve", f"x1b{h}_{i}")
            e.op("vector", lambda E: E.drain())
            if i >= 2:
                emit_out(i - 2)
        qdq(NT - 1)
        emit_out(NT - 2)
        emit_out(NT - 1)

    e.phase("F")
    e.wait("sync", "out0", f"outw{HALVES-1}_{6}")
    e.wait("sync", "out1", f"outw{HALVES-1}_{7}")
    e.wait("sync", "out2", f"outw{HALVES-1}_{5}")


def compute_psum_free(marks):
    freed = {}
    order = []
    for h in range(HALVES):
        for p in range(NP):
            order.append(((h, "B", p), p % 2, ("dve", f"zgB{h}_{p}")))
        for d_ in range(ND):
            order.append(((h, "C", d_), d_ % 2, ("dve", f"o1eC{h}_{d_}")))
        for i in range(NT):
            order.append(((h, "D", i), i % 2, ("dve", f"outD{h}_{i}")))
    last = {}
    for key, par, freemark in order:
        if par in last:
            sem, mk = last[par]
            freed[key] = (sem, marks.get(mk, 0))
        last[par] = freemark
    return freed


def build_program():
    nc = bass.Bass("TRN2", target_bir_lowering=False)
    t = {}
    t["x"] = nc.dram_tensor("x", [TCORE, D], F32, kind="ExternalInput")
    t["grep"] = nc.dram_tensor("grep", [128, D], F32, kind="ExternalInput")
    t["brep"] = nc.dram_tensor("brep", [128, D], F32, kind="ExternalInput")
    t["win"] = nc.dram_tensor("win", [2 * E, D], F32, kind="ExternalInput")
    t["wout"] = nc.dram_tensor("wout", [D, E], F32, kind="ExternalInput")
    t["w8"] = nc.dram_tensor("w8", [D, D], F8, kind="ExternalInput")
    t["dtrep"] = nc.dram_tensor("dtrep", [128, E], F32, kind="ExternalInput")
    t["dtcol"] = nc.dram_tensor("dtcol", [128, 32], F32, kind="ExternalInput")
    t["scol"] = nc.dram_tensor("scol", [128, 1], BF16, kind="ExternalInput")
    t["wscol"] = nc.dram_tensor("wscol", [128, 1], F32, kind="ExternalInput")
    t["out"] = nc.dram_tensor("out", [TCORE, D], F32, kind="ExternalOutput")
    t["winb"] = nc.dram_tensor("winb", [2 * E, D], BF16, kind="Internal")
    t["woutb"] = nc.dram_tensor("woutb", [D, E], BF16, kind="Internal")
    t["w3b"] = nc.dram_tensor("w3b", [D, D], BF16, kind="Internal")
    t["xnb"] = nc.dram_tensor("xnb", [HT, D], BF16, kind="Internal")
    t["o1b"] = nc.dram_tensor("o1b", [D, HT], BF16, kind="Internal")
    t["yhb"] = nc.dram_tensor("yhb", [HT, D], BF16, kind="Internal")

    with ExitStack() as ctx:
        en = ctx.enter_context
        t["st"] = en(nc.sbuf_tensor("st", [128, 16, 20], F32))
        t["dtw"] = en(nc.sbuf_tensor("dtw", [128, 32], F32))
        t["dtc"] = en(nc.sbuf_tensor("dtc", [128, 32], F32))
        t["cst"] = en(nc.sbuf_tensor("cst", [128, 4], F32))
        t["sm"] = en(nc.sbuf_tensor("sm", [128, 4], F32))
        t["scb"] = en(nc.sbuf_tensor("scb", [128, 1], BF16))
        t["gbf"] = en(nc.sbuf_tensor("gbf", [128, D], BF16))
        t["bbf"] = en(nc.sbuf_tensor("bbf", [128, D], BF16))
        t["XnT"] = en(nc.sbuf_tensor("XnT", [128, 16, HT], BF16))
        t["ATa"] = en(nc.sbuf_tensor("ATa", [128, 32768], BF16))
        t["WAa"] = en(nc.sbuf_tensor("WAa", [128, 32768], BF16))
        t["F32A"] = en(nc.sbuf_tensor("F32A", [128, 3, D], F32))
        t["ybuf"] = en(nc.sbuf_tensor("ybuf", [128, D], F32))
        t["SCR"] = en(nc.sbuf_tensor("SCR", [128, 6144], U8))
        t["P0"] = en(nc.psum_tensor("P0", [128, D], F32))
        t["P1"] = en(nc.psum_tensor("P1", [128, D], F32))

        sem_names = ["stp", "cast", "wzA", "wzB", "gsA", "gsB", "woA", "woB",
                     "w3t", "xnw", "xnt",
                     "o1wA", "o1wB", "ot0", "ot1", "ot2", "ot3", "ot4", "ot5",
                     "ot6", "ot7", "yhwA", "yhwB", "yhtA", "yhtB",
                     "ld0", "ld1", "ld2", "out0", "out1", "out2",
                     "pe", "dve", "act", "gp"]
        semh = {n: en(nc.semaphore(n)) for n in sem_names}

        t["psum_free"] = {}
        e1 = Em()
        emit(e1, t)
        t["psum_free"] = compute_psum_free(e1.marks)
        e1b = Em()
        emit(e1b, t)
        assert e1b.marks == e1.marks
        e2 = Em(marks=e1b.marks, marksem=e1b.marksem)
        emit(e2, t)
        assert e2.cnt == e1b.cnt

        for name, streams in e2.phases:
            with nc.Block() as block:
                for eng_name in ENGINES:
                    items = streams[eng_name]
                    if not items:
                        continue

                    def body(E, items=items):
                        for it in items:
                            if it[0] == "op":
                                _, fn, sem, k = it
                                ins = fn(E)
                                if sem is not None:
                                    ins.then_inc(semh[sem], k)
                            else:
                                _, sem, v = it
                                E.wait_ge(semh[sem], v)

                    getattr(block, eng_name)(body)

    return nc


def kernel(L0_attn_x, L0_attn_gamma, L0_attn_beta, L0_attn_W_in, L0_attn_W_out,
           L0_attn_dt, L0_ffn_weight, L0_ffn_weight_scale, L0_ffn_input_scale):
    X = np.ascontiguousarray(np.asarray(L0_attn_x).reshape(16384, D))
    gamma = np.ascontiguousarray(
        np.broadcast_to(np.asarray(L0_attn_gamma, np.float32).reshape(1, D), (128, D)))
    beta = np.ascontiguousarray(
        np.broadcast_to(np.asarray(L0_attn_beta, np.float32).reshape(1, D), (128, D)))
    win = np.ascontiguousarray(np.asarray(L0_attn_W_in, np.float32))
    wout = np.ascontiguousarray(np.asarray(L0_attn_W_out, np.float32))
    dtv = np.asarray(L0_attn_dt, np.float32).reshape(E)
    dtrep = np.ascontiguousarray(np.broadcast_to(dtv.reshape(1, E), (128, E)))
    dtcol = np.ascontiguousarray(dtv.reshape(32, 128).T)
    w8 = np.asarray(L0_ffn_weight).view(ml_dtypes.float8_e4m3)
    scol = np.ascontiguousarray(
        np.broadcast_to(np.asarray(L0_ffn_input_scale).reshape(1, 1), (128, 1)))
    wscol = np.ascontiguousarray(
        np.broadcast_to(np.asarray(L0_ffn_weight_scale, np.float32).reshape(1, 1),
                        (128, 1)))

    nc = build_program()
    in_maps = []
    for c in range(NCORE):
        in_maps.append(dict(
            x=np.ascontiguousarray(X[c * TCORE:(c + 1) * TCORE]),
            grep=gamma, brep=beta, win=win, wout=wout, w8=w8,
            dtrep=dtrep, dtcol=dtcol, scol=scol, wscol=wscol))
    res = run_bass_kernel_spmd(nc, in_maps, core_ids=list(range(NCORE)))
    full = np.concatenate([np.asarray(r["out"]) for r in res.results], axis=0)
    return full.reshape(4, 4096, D).astype(np.float32, copy=False)


# version 30
# speedup vs baseline: 1.0834x; 1.0632x over previous
import numpy as np
import ml_dtypes
import concourse.bass as bass
import concourse.mybir as mybir
from concourse.bass_utils import run_bass_kernel_spmd
from contextlib import ExitStack

F32 = mybir.dt.float32
BF16 = mybir.dt.bfloat16
F8 = mybir.dt.float8e4
U8 = mybir.dt.uint8
AF = mybir.ActivationFunctionType
ALU = mybir.AluOpType
AX = mybir.AxisListType

NCORE = 8
TCORE = 2048
HT = 1024
NT = 8
D = 2048
E = 4096
NP = 32
NG = 8
ND = 16
NDC = 4
EPS = 1e-5
RD = 1.0 / 2048.0

ENGINES = ("sync", "scalar", "vector", "gpsimd", "tensor")
HALVES = 2


class Em:

    def __init__(self, marks=None, marksem=None):
        self.final = marks is not None
        self.marks = {} if marks is None else marks
        self.marksem = {} if marksem is None else marksem
        self.cnt = {}
        self.phases = []
        self.cur = None

    def phase(self, name):
        self.cur = {e: [] for e in ENGINES}
        self.phases.append((name, self.cur))

    def op(self, eng, fn, sem=None, mark=None, k=1):
        n = None
        if sem is not None:
            self.cnt[sem] = self.cnt.get(sem, 0) + k
            n = self.cnt[sem]
            if mark is not None:
                if not self.final:
                    self.marks[mark] = n
                    self.marksem[mark] = sem
        if self.final:
            self.cur[eng].append(("op", fn, sem, k))

    def wait(self, eng, sem, mark=None, value=None):
        if self.final:
            if value is None:
                assert mark in self.marks, f"undefined mark {mark}"
                assert self.marksem[mark] == sem, (
                    f"mark {mark} is on sem {self.marksem[mark]}, waited on {sem}")
            v = value if value is not None else self.marks[mark]
            if v > 0:
                self.cur[eng].append(("wait", sem, v))

    def at(self, mark):
        return self.marks.get(mark, 0)


def emit(e, t):

    XnT = t["XnT"]
    ATa = t["ATa"]
    WAa = t["WAa"]
    F32A = t["F32A"]
    ybuf = t["ybuf"]
    o1e = ybuf.bitcast(BF16)
    SCR = t["SCR"]
    SCRb = SCR.bitcast(BF16)
    SCR8 = SCR.bitcast(F8)
    gbf = t["gbf"]; bbf = t["bbf"]
    dtw = t["dtw"]
    dtc = t["dtc"]
    cst = t["cst"]
    sm = t["sm"]
    scb = t["scb"]
    st = t["st"]
    P = (t["P0"], t["P1"])

    x = t["x"]; grep = t["grep"]; brep = t["brep"]
    win = t["win"]; wout = t["wout"]; w8 = t["w8"]
    dtrep = t["dtrep"]; out = t["out"]
    winb = t["winb"]; woutb = t["woutb"]; w3b = t["w3b"]
    xnb = t["xnb"]; o1b = t["o1b"]; yhb = t["yhb"]
    wscol = t["wscol"]

    def zs(par):
        return WAa[:, par * 8192:(par + 1) * 8192].rearrange("p (k e) -> p k e", k=16)

    def gs(par):
        return WAa[:, 16384 + par * 8192:16384 + (par + 1) * 8192].rearrange(
            "p (k e) -> p k e", k=16)

    def wo(par):
        return WAa[:, par * 16384:(par + 1) * 16384].rearrange("p (k e) -> p k e", k=32)

    w3T = WAa[:, 0:32768].rearrange("p (k o) -> p k o", k=16)

    def ATl(et):
        return ATa[:, et * 1024:(et + 1) * 1024]

    yhT = ATa[:, 0:8192].rearrange("p (u k c) -> p u k c", u=2, k=16)
    o1t = ATa[:, 8192:24576].rearrange("p (u c) -> p u c", u=8)
    lob = ATa[:, 24576:26624]
    hib = ATa[:, 26624:28672]
    yh = ATa[:, 28672:30720]
    xnt_stage = ATa[:, 30720:32768]

    zcb = SCRb[:, 0:1024]
    gcb = SCRb[:, 1024:2048]
    zgb = SCRb[:, 2048:3072]
    qlo = SCR8[:, 0:2048]
    qhi = SCR8[:, 2048:4096]
    msk = SCR[:, 4096:6144]

    xs = (F32A[:, 0, :], F32A[:, 1, :])
    txg = F32A[:, 2, :]
    x1s = (F32A[:, 0, :], F32A[:, 1, :], F32A[:, 2, :])
    dtrep_s = XnT.bitcast(F32).rearrange(
        "p a b -> p (a b)")[:, 0:4096]

    def f(ti, j):
        return st[:, ti, j:j + 1]

    e.phase("S")
    e.op("scalar", lambda E: E.dma_start(dtc[:, :], t["dtcol"][:, :]), "stp", k=16)
    e.op("scalar", lambda E: E.dma_start(scb[:, :], t["scol"][:, :]), "stp", k=16)
    e.op("scalar", lambda E: E.dma_start(cst[:, 3:4], wscol[:, :]), "stp", "ld_all", k=16)
    e.wait("vector", "stp", "ld_all")
    e.op("vector", lambda E: E.tensor_copy(cst[:, 2:3], scb[:, :]))
    e.op("vector", lambda E: E.drain())
    e.op("vector", lambda E: E.reciprocal(cst[:, 0:1], cst[:, 2:3]))
    e.op("vector", lambda E: E.drain())
    e.op("vector", lambda E: E.tensor_tensor(cst[:, 1:2], cst[:, 2:3], cst[:, 3:4], ALU.mult),
         "dve", "consts")
    e.op("gpsimd", lambda E: E.dma_start(dtrep_s[:, :], dtrep[:, :]), "cast", "ld_dt", k=16)
    e.wait("vector", "cast", "ld_dt")
    e.op("vector", lambda E: E.tensor_scalar(dtrep_s[:, :], dtrep_s[:, :], -1.0, None, ALU.mult))
    e.op("vector", lambda E: E.drain())
    e.op("vector", lambda E: E.reduce_max(sm[:, 0:1], dtrep_s[:, :], axis=AX.X))
    e.op("vector", lambda E: E.drain())
    e.op("vector", lambda E: E.tensor_scalar(sm[:, 1:2], sm[:, 0:1], -1.0, None, ALU.mult),
         "dve", "negm")
    e.wait("scalar", "dve", "negm")
    e.op("scalar", lambda E: E.activation(dtrep_s[:, :], dtrep_s[:, :], AF.Exp,
                                          bias=sm[:, 1:2]), "act", "expden")
    e.wait("vector", "act", "expden")
    e.op("vector", lambda E: E.reduce_sum(sm[:, 2:3], dtrep_s[:, :], axis=AX.X))
    e.op("vector", lambda E: E.drain())
    e.op("vector", lambda E: E.reciprocal(sm[:, 3:4], sm[:, 2:3]), "dve", "rs")
    e.wait("scalar", "stp", "ld_all")
    e.op("scalar", lambda E: E.activation(dtw[:, :], dtc[:, :], AF.Exp,
                                          scale=-1.0, bias=sm[:, 1:2]), "act", "expcol")
    e.wait("vector", "act", "expcol")
    e.op("vector", lambda E: E.drain())
    e.op("vector", lambda E: E.tensor_scalar(dtw[:, :], dtw[:, :], sm[:, 3:4], None,
                                             ALU.mult), "dve", "dtw")
    e.op("gpsimd", lambda E: E.dma_start(gbf[:, :], grep[:, :]), "cast", "cgb", k=16)
    e.op("gpsimd", lambda E: E.dma_start(bbf[:, :], brep[:, :]), "cast", "cbb", k=16)
    def emit_cast_zg(g):
        e.op("gpsimd", lambda E, g=g: E.dma_start(
            winb[512 * g:512 * (g + 1), :], win[512 * g:512 * (g + 1), :]),
            "cast", f"cz{g}", k=16)
        e.op("gpsimd", lambda E, g=g: E.dma_start(
            winb[4096 + 512 * g:4096 + 512 * (g + 1), :],
            win[4096 + 512 * g:4096 + 512 * (g + 1), :]), "cast", f"cg{g}", k=16)

    cast_steps = []
    for g in range(1, NG):
        cast_steps.append(("z", g))
        cast_steps.append(("g", g))
    for c in range(NDC):
        cast_steps.append(("w", c))
    for c in range(2):
        cast_steps.append(("3", c))

    def emit_cast_step(step):
        kind, idx = step
        if kind == "z":
            e.op("gpsimd", lambda E, g=idx: E.dma_start(
                winb[512 * g:512 * (g + 1), :], win[512 * g:512 * (g + 1), :]),
                "cast", f"cz{idx}", k=16)
        elif kind == "g":
            e.op("gpsimd", lambda E, g=idx: E.dma_start(
                winb[4096 + 512 * g:4096 + 512 * (g + 1), :],
                win[4096 + 512 * g:4096 + 512 * (g + 1), :]),
                "cast", f"cg{idx}", k=16)
        elif kind == "w":
            e.op("gpsimd", lambda E, c=idx: E.dma_start(
                woutb[512 * c:512 * (c + 1), :], wout[512 * c:512 * (c + 1), :]),
                "cast", f"cw{idx}", k=16)
        else:
            e.op("gpsimd", lambda E, c=idx: E.dma_start(
                w3b[1024 * c:1024 * (c + 1), :], w8[1024 * c:1024 * (c + 1), :]),
                "cast", f"c3{idx}", k=16)

    emit_cast_zg(0)

    for h in range(HALVES):
        hb = h * NT
        r0 = h * HT

        e.phase(f"A{h}")
        for i in range(NT):
            ti = hb + i
            if h == 1 and i < 2:
                e.wait("scalar", "out0", value=e.at(f"outw{0}_{6}"))
                e.wait("scalar", "out1", value=e.at(f"outw{0}_{7}"))
                e.wait("scalar", "out2", value=e.at(f"outw{0}_{5}"))
            if i >= 2:
                e.wait("scalar", "act", f"hAa{h}_{i-2}")
                e.wait("scalar", "dve", f"hAd{h}_{i-2}")
            e.op("scalar", lambda E, i=i, r0=r0: E.dma_start(
                xs[i % 2][:, :], x[r0 + i * 128:r0 + (i + 1) * 128, :]),
                f"ld{i % 2}", f"ldxA{h}_{i}", k=16)
            e.wait("scalar", f"ld{i % 2}", f"ldxA{h}_{i}")
            e.op("scalar", lambda E, i=i, ti=ti: E.activation(
                txg[:, :], xs[i % 2][:, :], AF.Square, accum_out=f(ti, 2)),
                "act", f"hAa{h}_{i}")
            e.wait("vector", f"ld{i % 2}", f"ldxA{h}_{i}")
            e.op("vector", lambda E, i=i, ti=ti: E.reduce_sum(
                f(ti, 0), xs[i % 2][:, :], axis=AX.X), "dve", f"hAd{h}_{i}")
            e.op("vector", lambda E: E.drain())

        def bv(fn, sem=None, mark=None, hb=hb):
            for q in range(NT):
                last = q == NT - 1
                e.op("vector", (lambda E, q=q, hb=hb: fn(E, hb + q)),
                     sem if last else None, mark if last else None)
            e.op("vector", lambda E: E.drain())

        def ba(fn, sem=None, mark=None, hb=hb):
            for q in range(NT):
                last = q == NT - 1
                e.op("scalar", (lambda E, q=q, hb=hb: fn(E, hb + q)),
                     sem if last else None, mark if last else None)

        e.wait("vector", "act", f"hAa{h}_{NT-1}")
        bv(lambda E, ti: E.tensor_scalar(f(ti, 1), f(ti, 0), RD, None, ALU.mult))
        bv(lambda E, ti: E.tensor_scalar(f(ti, 3), f(ti, 2), RD, None, ALU.mult))
        bv(lambda E, ti: E.tensor_tensor(f(ti, 4), f(ti, 1), f(ti, 1), ALU.mult))
        bv(lambda E, ti: E.tensor_tensor(f(ti, 5), f(ti, 3), f(ti, 4), ALU.subtract))
        bv(lambda E, ti: E.tensor_scalar(f(ti, 6), f(ti, 5), EPS, None, ALU.add),
           "dve", f"bA1{h}")
        e.wait("scalar", "dve", f"bA1{h}")
        ba(lambda E, ti: E.activation(f(ti, 7), f(ti, 6), AF.Sqrt), "act", f"bA1s{h}")
        e.wait("vector", "act", f"bA1s{h}")
        bv(lambda E, ti: E.reciprocal(f(ti, 8), f(ti, 7)))
        bv(lambda E, ti: E.tensor_tensor(f(ti, 4), f(ti, 8), f(ti, 8), ALU.mult))
        bv(lambda E, ti: E.tensor_tensor(f(ti, 4), f(ti, 4), f(ti, 6), ALU.mult))
        bv(lambda E, ti: E.tensor_scalar(f(ti, 4), f(ti, 4), -0.5, 1.5, ALU.mult, ALU.add))
        bv(lambda E, ti: E.tensor_tensor(f(ti, 8), f(ti, 8), f(ti, 4), ALU.mult))
        bv(lambda E, ti: E.tensor_scalar(f(ti, 9), f(ti, 8), 1.0, None, ALU.add))
        bv(lambda E, ti: E.scalar_tensor_tensor(
            f(ti, 10), f(ti, 1), -1.0, f(ti, 8), ALU.mult, ALU.mult))
        bv(lambda E, ti: E.tensor_tensor(f(ti, 11), f(ti, 8), f(ti, 8), ALU.mult))
        bv(lambda E, ti: E.tensor_tensor(f(ti, 11), f(ti, 11), f(ti, 5), ALU.mult))
        bv(lambda E, ti: E.tensor_scalar(f(ti, 11), f(ti, 11), EPS, None, ALU.add),
           "dve", f"bA2{h}")
        e.wait("scalar", "dve", f"bA2{h}")
        ba(lambda E, ti: E.activation(f(ti, 12), f(ti, 11), AF.Sqrt), "act", f"bA2s{h}")
        e.wait("vector", "act", f"bA2s{h}")
        bv(lambda E, ti: E.reciprocal(f(ti, 13), f(ti, 12)))
        bv(lambda E, ti: E.tensor_tensor(f(ti, 4), f(ti, 13), f(ti, 13), ALU.mult))
        bv(lambda E, ti: E.tensor_tensor(f(ti, 4), f(ti, 4), f(ti, 11), ALU.mult))
        bv(lambda E, ti: E.tensor_scalar(f(ti, 4), f(ti, 4), -0.5, 1.5, ALU.mult, ALU.add))
        bv(lambda E, ti: E.tensor_tensor(f(ti, 13), f(ti, 13), f(ti, 4), ALU.mult))
        bv(lambda E, ti: E.tensor_tensor(f(ti, 14), f(ti, 8), f(ti, 13), ALU.mult))
        bv(lambda E, ti: E.tensor_tensor(f(ti, 15), f(ti, 10), f(ti, 13), ALU.mult),
           "dve", f"bA{h}")

        tg = (txg, ybuf)
        xst = (xnt_stage, yh)
        for i in range(NT):
            ti = hb + i
            if i < 2:
                e.wait("scalar", "dve", f"bA{h}")
            else:
                e.wait("scalar", "act", f"tA{h}_{i-2}")
            e.op("scalar", lambda E, i=i, r0=r0: E.dma_start(
                xs[i % 2][:, :], x[r0 + i * 128:r0 + (i + 1) * 128, :]),
                f"ld{i % 2}", f"ldA2{h}_{i}", k=16)
            e.wait("scalar", f"ld{i % 2}", f"ldA2{h}_{i}")
            if i >= 2:
                e.wait("scalar", "gp", f"xnA{h}_{i-2}")
            e.op("scalar", lambda E, i=i, ti=ti: E.activation(
                tg[i % 2][:, :], xs[i % 2][:, :], AF.Identity,
                scale=f(ti, 14), bias=f(ti, 15)), "act", f"tA{h}_{i}")
            e.wait("vector", "act", f"tA{h}_{i}")
            if h == 0 and i == 0:
                e.wait("vector", "cast", "cgb")
            e.op("vector", lambda E, i=i: E.tensor_tensor(
                tg[i % 2][:, :], tg[i % 2][:, :], gbf[:, :], ALU.mult),
                "dve", f"xgA{h}_{i}")
            e.wait("gpsimd", "dve", f"xgA{h}_{i}")
            if h == 0 and i == 0:
                e.wait("gpsimd", "cast", "cbb")
            if i >= 2:
                e.wait("gpsimd", "xnw", f"xnw{h}_{i-2}")
            if h == 1 and i == 0:
                e.wait("gpsimd", "pe", f"peC{0}_{15}")
            e.op("gpsimd", lambda E, i=i: E.tensor_tensor(
                xst[i % 2][:, :], tg[i % 2][:, :], bbf[:, :], ALU.add),
                "gp", f"xnA{h}_{i}")
            e.wait("sync", "gp", f"xnA{h}_{i}")
            e.op("sync", lambda E, i=i: E.dma_start(
                xnb[i * 128:(i + 1) * 128, :], xst[i % 2][:, :]),
                "xnw", f"xnw{h}_{i}", k=16)
        e.wait("scalar", "xnw", f"xnw{h}_{7}")
        if h == 0:
            e.wait("scalar", "dve", "rs")
        if h == 1:
            e.wait("scalar", "pe", f"peB{0}_{NP-1}")
        for dp in range(16):
            e.op("scalar", lambda E, dp=dp: E.dma_start(
                XnT[:, dp, :], xnb[0:HT, dp * 128:(dp + 1) * 128], transpose=True),
                "xnt", f"xnt{h}_{dp}", k=16)

        e.phase(f"B{h}")
        for g in range(NG):
            par = g % 2
            if h == 0:
                e.wait("sync", "cast", f"cz{g}")
                e.wait("sync", "cast", f"cg{g}")
            if h == 1 and g < 2:
                e.wait("sync", "pe", f"peD{0}_{7}")
            if g >= 2:
                e.wait("sync", "pe", f"peB{h}_{4*(g-2)+3}")
            for dp in range(16):
                e.op("sync", lambda E, g=g, dp=dp, par=par: E.dma_start(
                    zs(par)[:, dp, :],
                    winb[512 * g:512 * (g + 1), dp * 128:(dp + 1) * 128],
                    transpose=True), "wzA" if par == 0 else "wzB", k=16)
            for dp in range(16):
                e.op("sync", lambda E, g=g, dp=dp, par=par: E.dma_start(
                    gs(par)[:, dp, :],
                    winb[4096 + 512 * g:4096 + 512 * (g + 1), dp * 128:(dp + 1) * 128],
                    transpose=True), "wzA" if par == 0 else "wzB",
                    f"wz{h}_{g}" if dp == 15 else None, k=16)

        for p in range(NP):
            g = p // 4
            par = g % 2
            sl = (p % 4) * 128
            if p % 4 == 0:
                e.wait("tensor", "wzA" if par == 0 else "wzB", f"wz{h}_{g}")
            if p == 0:
                e.wait("tensor", "xnt", f"xnt{h}_{15}")
                e.wait("tensor", "dve", "dtw")
            pf = t["psum_free"].get((h, "B", p))
            if pf:
                e.wait("tensor", pf[0], value=pf[1])
            for k in range(16):
                for tc in range(2):
                    e.op("tensor", lambda E, p=p, k=k, tc=tc, par=par, sl=sl: E.matmul(
                        P[p % 2][:, tc * 512:(tc + 1) * 512],
                        zs(par)[:, k, sl:sl + 128],
                        XnT[:, k, tc * 512:(tc + 1) * 512],
                        start=(k == 0), stop=(k == 15)))
            for k in range(16):
                for tc in range(2):
                    e.op("tensor", lambda E, p=p, k=k, tc=tc, par=par, sl=sl: E.matmul(
                        P[p % 2][:, 1024 + tc * 512:1024 + (tc + 1) * 512],
                        gs(par)[:, k, sl:sl + 128],
                        XnT[:, k, tc * 512:(tc + 1) * 512],
                        start=(k == 0), stop=(k == 15)),
                        sem="pe" if (k == 15 and tc == 1) else None,
                        mark=f"peB{h}_{p}" if (k == 15 and tc == 1) else None)

            e.wait("vector", "pe", f"peB{h}_{p}")
            if p >= 1:
                e.wait("vector", "gp", f"mulB{h}_{p-1}")
            e.op("vector", lambda E, p=p: E.tensor_scalar(
                zcb[:, :], P[p % 2][:, 0:1024], -15.0, 15.0, ALU.max, ALU.min),
                "dve", f"czB{h}_{p}")
            e.op("vector", lambda E, p=p: E.tensor_scalar(
                gcb[:, :], P[p % 2][:, 1024:2048], -15.0, 15.0, ALU.max, ALU.min),
                "dve", f"cgB{h}_{p}")
            e.wait("vector", "act", f"sigB{h}_{p}")
            e.op("vector", lambda E, p=p: E.tensor_tensor(
                zgb[:, :], P[p % 2][:, 0:1024], zcb[:, :], ALU.mult),
                "dve", f"zgB{h}_{p}")
            e.op("vector", lambda E: E.drain())
            e.wait("scalar", "dve", f"czB{h}_{p}")
            e.op("scalar", lambda E: E.activation(zcb[:, :], zcb[:, :], AF.Sigmoid),
                 "act", f"sigB{h}_{p}")
            e.wait("scalar", "dve", f"cgB{h}_{p}")
            e.wait("scalar", "dve", f"zgB{h}_{p}")
            e.op("scalar", lambda E: E.activation(zcb[:, :], gcb[:, :], AF.Sigmoid),
                 "act", f"sigG{h}_{p}")
            e.wait("gpsimd", "act", f"sigG{h}_{p}")
            if h == 1 and p == 0:
                e.wait("gpsimd", "dve", f"selD{0}_{7}")
            e.op("gpsimd", lambda E: E.tensor_tensor(
                gcb[:, :], gcb[:, :], zcb[:, :], ALU.mult), "gp", f"mulB{h}_{p}")
            if h == 0 and p < len(cast_steps):
                emit_cast_step(cast_steps[p])
            e.wait("vector", "gp", f"mulB{h}_{p}")
            if h == 1 and p == 0:
                e.wait("vector", "xnw", f"xnw{1}_{7}")
                e.wait("vector", "pe", f"peD{0}_{7}")
                e.wait("vector", "yhwB", f"yhw{0}_{7}")
            if h == 0 and p == 0:
                e.wait("vector", "dve", "dtw")
            e.op("vector", lambda E, p=p: E.scalar_tensor_tensor(
                ATl(p)[:, :], zgb[:, :], dtw[:, p:p + 1], gcb[:, :],
                ALU.mult, ALU.mult), "dve", f"sttB{h}_{p}")

        e.phase(f"C{h}")
        def emit_wo(c):
            par = c % 2
            q = "sync" if par == 0 else "scalar"
            if h == 0:
                e.wait(q, "cast", f"cw{c}")
            if c < 2:
                e.wait(q, "pe", f"peB{h}_{27 if c == 0 else 31}")
            else:
                e.wait(q, "pe", f"peC{h}_{4*(c-2)+3}")
            for ep in range(32):
                e.op(q, lambda E, c=c, ep=ep, par=par: E.dma_start(
                    wo(par)[:, ep, :],
                    woutb[512 * c:512 * (c + 1), ep * 128:(ep + 1) * 128],
                    transpose=True), "woA" if par == 0 else "woB",
                    f"wo{h}_{c}" if ep == 31 else None, k=16)

        emit_wo(0)
        emit_wo(1)
        emit_wo(2)
        for dt_ in range(ND):
            if dt_ == 8:
                emit_wo(3)
            c = dt_ // 4
            par = c % 2
            sl = (dt_ % 4) * 128
            if dt_ % 4 == 0:
                e.wait("tensor", "woA" if par == 0 else "woB", f"wo{h}_{c}")
            if dt_ == 0:
                e.wait("tensor", "dve", f"sttB{h}_{NP-1}")
            pf = t["psum_free"].get((h, "C", dt_))
            if pf:
                e.wait("tensor", pf[0], value=pf[1])
            for ep in range(32):
                for tc in range(2):
                    e.op("tensor", lambda E, dt_=dt_, ep=ep, tc=tc, par=par, sl=sl: E.matmul(
                        P[dt_ % 2][:, tc * 512:(tc + 1) * 512],
                        wo(par)[:, ep, sl:sl + 128],
                        ATl(ep)[:, tc * 512:(tc + 1) * 512],
                        start=(ep == 0), stop=(ep == 31)),
                        sem="pe" if (ep == 31 and tc == 1) else None,
                        mark=f"peC{h}_{dt_}" if (ep == 31 and tc == 1) else None)
            e.wait("vector", "pe", f"peC{h}_{dt_}")
            if dt_ >= 2:
                e.wait("vector", "o1wA" if dt_ % 2 == 0 else "o1wB", f"o1w{h}_{dt_-2}")
            e.op("vector", lambda E, dt_=dt_: E.tensor_copy(
                o1e[:, (dt_ % 2) * 1024:(dt_ % 2 + 1) * 1024], P[dt_ % 2][:, 0:1024]),
                "dve", f"o1eC{h}_{dt_}")
            e.wait("scalar", "dve", f"o1eC{h}_{dt_}")
            e.op("scalar", lambda E, dt_=dt_: E.dma_start(
                o1b[dt_ * 128:(dt_ + 1) * 128, :],
                o1e[:, (dt_ % 2) * 1024:(dt_ % 2 + 1) * 1024]),
                "o1wA" if dt_ % 2 == 0 else "o1wB", f"o1w{h}_{dt_}", k=16)

        e.phase(f"D{h}")
        e.wait("sync", "pe", f"peC{h}_{15}")
        e.wait("sync", "o1wA", f"o1w{h}_{14}")
        e.wait("sync", "o1wB", f"o1w{h}_{15}")
        for i in range(NT):
            e.op("sync", lambda E, i=i: E.dma_start(
                o1t[:, i, :], o1b[0:2048, i * 128:(i + 1) * 128], transpose=True),
                f"ot{i}", f"o1t{h}_{i}", k=16)
        if h == 0:
            e.wait("sync", "cast", "c31")
        for k in range(16):
            e.op("sync", lambda E, k=k: E.dma_start(
                w3T[:, k, :], w3b[0:2048, k * 128:(k + 1) * 128], transpose=True),
                "w3t", f"w3t{h}_{k}" if k == 15 else None, k=16)

        for i in range(NT):
            ti = hb + i
            if i == 0:
                e.wait("scalar", "act", f"tA{h}_{NT-2}")
            elif i == 1:
                e.wait("scalar", "act", f"tA{h}_{NT-1}")
            else:
                e.wait("scalar", "dve", f"hDd{h}_{i-2}")
                e.wait("scalar", "act", f"hDa{h}_{i-2}")
            e.op("scalar", lambda E, i=i, r0=r0: E.dma_start(
                xs[i % 2][:, :], x[r0 + i * 128:r0 + (i + 1) * 128, :]),
                f"ld{i % 2}", f"ldxD{h}_{i}", k=16)
            e.wait("scalar", f"ld{i % 2}", f"ldxD{h}_{i}")
            e.op("scalar", lambda E, i=i, ti=ti: E.activation(
                xs[i % 2][:, :], xs[i % 2][:, :], AF.Identity,
                scale=f(ti, 9), bias=f(ti, 10)), "act", f"xaD{h}_{i}")
            if i >= 1:
                e.wait("scalar", "dve", f"x1D{h}_{i-1}")
                e.op("scalar", lambda E, i=i, ti=ti: E.activation(
                    ybuf[:, :], xs[(i - 1) % 2][:, :], AF.Square,
                    accum_out=f(ti - 1, 16)), "act", f"hDa{h}_{i-1}")
            e.wait("vector", "act", f"xaD{h}_{i}")
            e.wait("vector", f"ot{i}", f"o1t{h}_{i}")
            e.op("vector", lambda E, i=i: E.tensor_tensor(
                xs[i % 2][:, :], xs[i % 2][:, :], o1t[:, i, :], ALU.add),
                "dve", f"x1D{h}_{i}")
            e.op("vector", lambda E: E.drain())
            e.op("vector", lambda E, i=i, ti=ti: E.reduce_sum(
                f(ti, 17), xs[i % 2][:, :], axis=AX.X), "dve", f"hDd{h}_{i}")
            e.op("vector", lambda E: E.drain())
        e.wait("scalar", "dve", f"x1D{h}_{NT-1}")
        e.op("scalar", lambda E, ti=hb + NT - 1: E.activation(
            ybuf[:, :], xs[(NT - 1) % 2][:, :], AF.Square,
            accum_out=f(ti, 16)), "act", f"hDa{h}_{NT-1}")

        def bv(fn, sem=None, mark=None, hb=hb):
            for q in range(NT):
                last = q == NT - 1
                e.op("vector", (lambda E, q=q, hb=hb: fn(E, hb + q)),
                     sem if last else None, mark if last else None)
            e.op("vector", lambda E: E.drain())

        def ba(fn, sem=None, mark=None, hb=hb):
            for q in range(NT):
                last = q == NT - 1
                e.op("scalar", (lambda E, q=q, hb=hb: fn(E, hb + q)),
                     sem if last else None, mark if last else None)

        e.wait("vector", "act", f"hDa{h}_{NT-1}")
        if h == 0:
            e.wait("vector", "dve", "consts")
        bv(lambda E, ti: E.tensor_scalar(f(ti, 18), f(ti, 17), RD, None, ALU.mult))
        bv(lambda E, ti: E.tensor_scalar(f(ti, 19), f(ti, 16), RD, None, ALU.mult))
        bv(lambda E, ti: E.tensor_tensor(f(ti, 4), f(ti, 18), f(ti, 18), ALU.mult))
        bv(lambda E, ti: E.tensor_tensor(f(ti, 19), f(ti, 19), f(ti, 4), ALU.subtract))
        bv(lambda E, ti: E.tensor_scalar(f(ti, 19), f(ti, 19), EPS, None, ALU.add),
           "dve", f"bD1{h}")
        e.wait("scalar", "dve", f"bD1{h}")
        ba(lambda E, ti: E.activation(f(ti, 5), f(ti, 19), AF.Sqrt), "act", f"bD1s{h}")
        e.wait("vector", "act", f"bD1s{h}")
        bv(lambda E, ti: E.reciprocal(f(ti, 6), f(ti, 5)))
        bv(lambda E, ti: E.tensor_tensor(f(ti, 4), f(ti, 6), f(ti, 6), ALU.mult))
        bv(lambda E, ti: E.tensor_tensor(f(ti, 4), f(ti, 4), f(ti, 19), ALU.mult))
        bv(lambda E, ti: E.tensor_scalar(f(ti, 4), f(ti, 4), -0.5, 1.5, ALU.mult, ALU.add))
        bv(lambda E, ti: E.tensor_tensor(f(ti, 6), f(ti, 6), f(ti, 4), ALU.mult))
        bv(lambda E, ti: E.tensor_scalar(f(ti, 7), f(ti, 6), cst[:, 0:1], None, ALU.mult))
        bv(lambda E, ti: E.scalar_tensor_tensor(
            f(ti, 8), f(ti, 18), -1.0, f(ti, 7), ALU.mult, ALU.mult), "dve", f"bD{h}")

        def emit_out(i):
            e.wait("vector", "pe", f"peD{h}_{i}")
            e.op("vector", lambda E, i=i: E.scalar_tensor_tensor(
                x1s[i % 3][:, :], P[i % 2][:, :], cst[:, 1:2], x1s[i % 3][:, :],
                ALU.mult, ALU.add), "dve", f"outD{h}_{i}")
            e.wait("scalar", "dve", f"outD{h}_{i}")
            e.op("scalar", lambda E, i=i, r0=r0: E.dma_start(
                out[r0 + i * 128:r0 + (i + 1) * 128, :], x1s[i % 3][:, :]),
                f"out{i % 3}", f"outw{h}_{i}", k=16)

        XnTb = XnT.rearrange("p a b -> p (a b)")
        XnT8 = XnT.bitcast(F8)
        XnTu = XnT.bitcast(U8)
        ybs = (ybuf, XnTb[:, 4 * 1024:8 * 1024].bitcast(F32))
        lobs = (lob, XnTb[:, 0:2048])
        hibs = (hib, XnTb[:, 2048:4096])
        yhs = (yh, XnTb[:, 12 * 1024:14 * 1024])
        msks = (msk, XnTu[:, 8, :])
        qlos = (qlo, XnT8[:, 9, :])
        qhis = (qhi, XnT8[:, 10, :])

        def qdq(j):
            q = j % 2
            u = j // 2
            e.wait("scalar", "dve", f"x1b{h}_{j}")
            if j >= 2:
                e.wait("scalar", "dve", f"upD{h}_{j-2}")
                e.wait("scalar", "dve", f"selD{h}_{j-2}")
            e.op("scalar", lambda E, j=j, ti=hb + j, q=q: E.activation(
                ybs[q][:, :], x1s[j % 3][:, :], AF.Identity,
                scale=f(ti, 7), bias=f(ti, 8)), "act", f"yD{h}_{j}")
            e.wait("vector", "act", f"yD{h}_{j}")
            e.op("vector", lambda E, q=q: E.tensor_scalar(
                ybs[q][:, :], ybs[q][:, :], -448.0, 448.0, ALU.max, ALU.min),
                "dve", f"vcD{h}_{j}")
            e.wait("scalar", "dve", f"vcD{h}_{j}")
            e.op("scalar", lambda E, q=q: E.activation(
                qlos[q][:, :], ybs[q][:, :], AF.Copy))
            e.op("scalar", lambda E, q=q: E.activation(
                qhis[q][:, :], ybs[q][:, :], AF.Copy, scale=0.5))
            e.op("scalar", lambda E: E.drain())
            e.op("scalar", lambda E, q=q: E.activation(
                msks[q][:, :], qlos[q][:, :], AF.Is_finite), "act", f"m8D{h}_{j}")
            e.wait("vector", "act", f"m8D{h}_{j}")
            e.op("vector", lambda E, q=q: E.tensor_copy(lobs[q][:, :], qlos[q][:, :]))
            e.op("vector", lambda E, q=q: E.tensor_scalar(
                hibs[q][:, :], qhis[q][:, :], 2.0, None, ALU.mult), "dve", f"upD{h}_{j}")
            e.op("vector", lambda E: E.drain())
            if j >= 2:
                e.wait("vector", "yhwA" if ((j - 2) // 2) % 2 == 0 else "yhwB",
                       f"yhw{h}_{j-2}")
            e.op("vector", lambda E, q=q: E.select(
                yhs[q][:, :], msks[q][:, :], lobs[q][:, :], hibs[q][:, :]),
                "dve", f"selD{h}_{j}")
            e.wait("scalar", "dve", f"selD{h}_{j}")
            e.op("scalar", lambda E, j=j, q=q: E.dma_start(
                yhb[j * 128:(j + 1) * 128, :], yhs[q][:, :]),
                "yhwA" if u % 2 == 0 else "yhwB", f"yhw{h}_{j}", k=16)
            e.wait("sync", "yhwA" if u % 2 == 0 else "yhwB", f"yhw{h}_{j}")
            if j % 2 == 0 and u >= 2:
                e.wait("sync", "pe", f"peD{h}_{2*(u-2)+1}")
            for k in range(16):
                e.op("sync", lambda E, j=j, u=u, k=k: E.dma_start(
                    yhT[:, u % 2, k, (j % 2) * 128:(j % 2) * 128 + 128],
                    yhb[j * 128:(j + 1) * 128, k * 128:(k + 1) * 128],
                    transpose=True), "yhtA" if u % 2 == 0 else "yhtB",
                    f"yht{h}_{j}" if k == 15 else None, k=16)
            e.wait("tensor", "yhtA" if u % 2 == 0 else "yhtB", f"yht{h}_{j}")
            if j == 0:
                e.wait("tensor", "w3t", f"w3t{h}_{15}")
            pf = t["psum_free"].get((h, "D", j))
            if pf:
                e.wait("tensor", pf[0], value=pf[1])
            for k in range(16):
                for oc in range(4):
                    e.op("tensor", lambda E, j=j, u=u, k=k, oc=oc: E.matmul(
                        P[j % 2][:, oc * 512:(oc + 1) * 512],
                        yhT[:, u % 2, k, (j % 2) * 128:(j % 2) * 128 + 128],
                        w3T[:, k, oc * 512:(oc + 1) * 512],
                        start=(k == 0), stop=(k == 15)),
                        sem="pe" if (k == 15 and oc == 3) else None,
                        mark=f"peD{h}_{j}" if (k == 15 and oc == 3) else None)

        for i in range(NT):
            ti = hb + i
            if i < 3:
                e.wait("scalar", "dve", f"bD{h}")
            else:
                e.wait("scalar", f"out{(i - 3) % 3}", f"outw{h}_{i-3}")
            e.op("scalar", lambda E, i=i, r0=r0: E.dma_start(
                x1s[i % 3][:, :], x[r0 + i * 128:r0 + (i + 1) * 128, :]),
                f"ld{i % 3}", f"ldD2{h}_{i}", k=16)
            if i >= 1:
                qdq(i - 1)
            e.wait("scalar", f"ld{i % 3}", f"ldD2{h}_{i}")
            e.op("scalar", lambda E, i=i, ti=ti: E.activation(
                x1s[i % 3][:, :], x1s[i % 3][:, :], AF.Identity,
                scale=f(ti, 9), bias=f(ti, 10)), "act", f"xbD{h}_{i}")
            e.wait("vector", "act", f"xbD{h}_{i}")
            e.wait("vector", f"ot{i}", f"o1t{h}_{i}")
            e.op("vector", lambda E, i=i: E.tensor_tensor(
                x1s[i % 3][:, :], x1s[i % 3][:, :], o1t[:, i, :], ALU.add),
                "dve", f"x1b{h}_{i}")
            e.op("vector", lambda E: E.drain())
            if i >= 2:
                emit_out(i - 2)
        qdq(NT - 1)
        emit_out(NT - 2)
        emit_out(NT - 1)

    e.phase("F")
    e.wait("sync", "out0", f"outw{HALVES-1}_{6}")
    e.wait("sync", "out1", f"outw{HALVES-1}_{7}")
    e.wait("sync", "out2", f"outw{HALVES-1}_{5}")


def compute_psum_free(marks):
    freed = {}
    order = []
    for h in range(HALVES):
        for p in range(NP):
            order.append(((h, "B", p), p % 2, ("dve", f"zgB{h}_{p}")))
        for d_ in range(ND):
            order.append(((h, "C", d_), d_ % 2, ("dve", f"o1eC{h}_{d_}")))
        for i in range(NT):
            order.append(((h, "D", i), i % 2, ("dve", f"outD{h}_{i}")))
    last = {}
    for key, par, freemark in order:
        if par in last:
            sem, mk = last[par]
            freed[key] = (sem, marks.get(mk, 0))
        last[par] = freemark
    return freed


def build_program():
    nc = bass.Bass("TRN2", target_bir_lowering=False)
    t = {}
    t["x"] = nc.dram_tensor("x", [TCORE, D], F32, kind="ExternalInput")
    t["grep"] = nc.dram_tensor("grep", [128, D], F32, kind="ExternalInput")
    t["brep"] = nc.dram_tensor("brep", [128, D], F32, kind="ExternalInput")
    t["win"] = nc.dram_tensor("win", [2 * E, D], F32, kind="ExternalInput")
    t["wout"] = nc.dram_tensor("wout", [D, E], F32, kind="ExternalInput")
    t["w8"] = nc.dram_tensor("w8", [D, D], F8, kind="ExternalInput")
    t["dtrep"] = nc.dram_tensor("dtrep", [128, E], F32, kind="ExternalInput")
    t["dtcol"] = nc.dram_tensor("dtcol", [128, 32], F32, kind="ExternalInput")
    t["scol"] = nc.dram_tensor("scol", [128, 1], BF16, kind="ExternalInput")
    t["wscol"] = nc.dram_tensor("wscol", [128, 1], F32, kind="ExternalInput")
    t["out"] = nc.dram_tensor("out", [TCORE, D], F32, kind="ExternalOutput")
    t["winb"] = nc.dram_tensor("winb", [2 * E, D], BF16, kind="Internal")
    t["woutb"] = nc.dram_tensor("woutb", [D, E], BF16, kind="Internal")
    t["w3b"] = nc.dram_tensor("w3b", [D, D], BF16, kind="Internal")
    t["xnb"] = nc.dram_tensor("xnb", [HT, D], BF16, kind="Internal")
    t["o1b"] = nc.dram_tensor("o1b", [D, HT], BF16, kind="Internal")
    t["yhb"] = nc.dram_tensor("yhb", [HT, D], BF16, kind="Internal")

    with ExitStack() as ctx:
        en = ctx.enter_context
        t["st"] = en(nc.sbuf_tensor("st", [128, 16, 20], F32))
        t["dtw"] = en(nc.sbuf_tensor("dtw", [128, 32], F32))
        t["dtc"] = en(nc.sbuf_tensor("dtc", [128, 32], F32))
        t["cst"] = en(nc.sbuf_tensor("cst", [128, 4], F32))
        t["sm"] = en(nc.sbuf_tensor("sm", [128, 4], F32))
        t["scb"] = en(nc.sbuf_tensor("scb", [128, 1], BF16))
        t["gbf"] = en(nc.sbuf_tensor("gbf", [128, D], BF16))
        t["bbf"] = en(nc.sbuf_tensor("bbf", [128, D], BF16))
        t["XnT"] = en(nc.sbuf_tensor("XnT", [128, 16, HT], BF16))
        t["ATa"] = en(nc.sbuf_tensor("ATa", [128, 32768], BF16))
        t["WAa"] = en(nc.sbuf_tensor("WAa", [128, 32768], BF16))
        t["F32A"] = en(nc.sbuf_tensor("F32A", [128, 3, D], F32))
        t["ybuf"] = en(nc.sbuf_tensor("ybuf", [128, D], F32))
        t["SCR"] = en(nc.sbuf_tensor("SCR", [128, 6144], U8))
        t["P0"] = en(nc.psum_tensor("P0", [128, D], F32))
        t["P1"] = en(nc.psum_tensor("P1", [128, D], F32))

        sem_names = ["stp", "cast", "wzA", "wzB", "gsA", "gsB", "woA", "woB",
                     "w3t", "xnw", "xnt",
                     "o1wA", "o1wB", "ot0", "ot1", "ot2", "ot3", "ot4", "ot5",
                     "ot6", "ot7", "yhwA", "yhwB", "yhtA", "yhtB",
                     "ld0", "ld1", "ld2", "out0", "out1", "out2",
                     "pe", "dve", "act", "gp"]
        semh = {n: en(nc.semaphore(n)) for n in sem_names}

        t["psum_free"] = {}
        e1 = Em()
        emit(e1, t)
        t["psum_free"] = compute_psum_free(e1.marks)
        e1b = Em()
        emit(e1b, t)
        assert e1b.marks == e1.marks
        e2 = Em(marks=e1b.marks, marksem=e1b.marksem)
        emit(e2, t)
        assert e2.cnt == e1b.cnt

        for name, streams in e2.phases:
            with nc.Block() as block:
                for eng_name in ENGINES:
                    items = streams[eng_name]
                    if not items:
                        continue

                    def body(E, items=items):
                        for it in items:
                            if it[0] == "op":
                                _, fn, sem, k = it
                                ins = fn(E)
                                if sem is not None:
                                    ins.then_inc(semh[sem], k)
                            else:
                                _, sem, v = it
                                E.wait_ge(semh[sem], v)

                    getattr(block, eng_name)(body)

    return nc


def kernel(L0_attn_x, L0_attn_gamma, L0_attn_beta, L0_attn_W_in, L0_attn_W_out,
           L0_attn_dt, L0_ffn_weight, L0_ffn_weight_scale, L0_ffn_input_scale):
    X = np.ascontiguousarray(np.asarray(L0_attn_x).reshape(16384, D))
    gamma = np.ascontiguousarray(
        np.broadcast_to(np.asarray(L0_attn_gamma, np.float32).reshape(1, D), (128, D)))
    beta = np.ascontiguousarray(
        np.broadcast_to(np.asarray(L0_attn_beta, np.float32).reshape(1, D), (128, D)))
    win = np.ascontiguousarray(np.asarray(L0_attn_W_in, np.float32))
    wout = np.ascontiguousarray(np.asarray(L0_attn_W_out, np.float32))
    dtv = np.asarray(L0_attn_dt, np.float32).reshape(E)
    dtrep = np.ascontiguousarray(np.broadcast_to(dtv.reshape(1, E), (128, E)))
    dtcol = np.ascontiguousarray(dtv.reshape(32, 128).T)
    w8 = np.asarray(L0_ffn_weight).view(ml_dtypes.float8_e4m3)
    scol = np.ascontiguousarray(
        np.broadcast_to(np.asarray(L0_ffn_input_scale).reshape(1, 1), (128, 1)))
    wscol = np.ascontiguousarray(
        np.broadcast_to(np.asarray(L0_ffn_weight_scale, np.float32).reshape(1, 1),
                        (128, 1)))

    nc = build_program()
    in_maps = []
    for c in range(NCORE):
        in_maps.append(dict(
            x=np.ascontiguousarray(X[c * TCORE:(c + 1) * TCORE]),
            grep=gamma, brep=beta, win=win, wout=wout, w8=w8,
            dtrep=dtrep, dtcol=dtcol, scol=scol, wscol=wscol))
    res = run_bass_kernel_spmd(nc, in_maps, core_ids=list(range(NCORE)))
    full = np.concatenate([np.asarray(r["out"]) for r in res.results], axis=0)
    return full.reshape(4, 4096, D).astype(np.float32, copy=False)


# version 33 (final)
# speedup vs baseline: 1.1001x; 1.0154x over previous
import numpy as np
import ml_dtypes
import concourse.bass as bass
import concourse.mybir as mybir
from concourse.bass_utils import run_bass_kernel_spmd
from contextlib import ExitStack

F32 = mybir.dt.float32
BF16 = mybir.dt.bfloat16
F8 = mybir.dt.float8e4
U8 = mybir.dt.uint8
AF = mybir.ActivationFunctionType
ALU = mybir.AluOpType
AX = mybir.AxisListType

NCORE = 8
TCORE = 2048
HT = 1024
NT = 8
D = 2048
E = 4096
NP = 32
NG = 8
ND = 16
NDC = 4
EPS = 1e-5
RD = 1.0 / 2048.0

ENGINES = ("sync", "scalar", "vector", "gpsimd", "tensor")
HALVES = 2


class Em:

    def __init__(self, marks=None, marksem=None):
        self.final = marks is not None
        self.marks = {} if marks is None else marks
        self.marksem = {} if marksem is None else marksem
        self.cnt = {}
        self.phases = []
        self.cur = None

    def phase(self, name):
        self.cur = {e: [] for e in ENGINES}
        self.phases.append((name, self.cur))

    def op(self, eng, fn, sem=None, mark=None, k=1):
        n = None
        if sem is not None:
            self.cnt[sem] = self.cnt.get(sem, 0) + k
            n = self.cnt[sem]
            if mark is not None:
                if not self.final:
                    self.marks[mark] = n
                    self.marksem[mark] = sem
        if self.final:
            self.cur[eng].append(("op", fn, sem, k))

    def wait(self, eng, sem, mark=None, value=None):
        if self.final:
            if value is None:
                assert mark in self.marks, f"undefined mark {mark}"
                assert self.marksem[mark] == sem, (
                    f"mark {mark} is on sem {self.marksem[mark]}, waited on {sem}")
            v = value if value is not None else self.marks[mark]
            if v > 0:
                self.cur[eng].append(("wait", sem, v))

    def at(self, mark):
        return self.marks.get(mark, 0)


def emit(e, t):

    XnT = t["XnT"]
    ATa = t["ATa"]
    WAa = t["WAa"]
    F32A = t["F32A"]
    ybuf = t["ybuf"]
    o1e = ybuf.bitcast(BF16)
    SCR = t["SCR"]
    SCRb = SCR.bitcast(BF16)
    SCR8 = SCR.bitcast(F8)
    gbf = t["gbf"]; bbf = t["bbf"]
    dtw = t["dtw"]
    dtc = t["dtc"]
    cst = t["cst"]
    sm = t["sm"]
    scb = t["scb"]
    st = t["st"]
    P = (t["P0"], t["P1"])

    x = t["x"]; grep = t["grep"]; brep = t["brep"]
    win = t["win"]; wout = t["wout"]; w8 = t["w8"]
    dtrep = t["dtrep"]; out = t["out"]
    winb = t["winb"]; woutb = t["woutb"]; w3b = t["w3b"]
    xnb = t["xnb"]; o1b = t["o1b"]; yhb = t["yhb"]
    wscol = t["wscol"]

    def zs(par):
        return WAa[:, par * 8192:(par + 1) * 8192].rearrange("p (k e) -> p k e", k=16)

    def gs(par):
        return WAa[:, 16384 + par * 8192:16384 + (par + 1) * 8192].rearrange(
            "p (k e) -> p k e", k=16)

    def wo(par):
        return WAa[:, par * 16384:(par + 1) * 16384].rearrange("p (k e) -> p k e", k=32)

    w3T = WAa[:, 0:32768].rearrange("p (k o) -> p k o", k=16)

    def ATl(et):
        return ATa[:, et * 1024:(et + 1) * 1024]

    yhT = ATa[:, 0:8192].rearrange("p (u k c) -> p u k c", u=2, k=16)
    o1t = ATa[:, 8192:24576].rearrange("p (u c) -> p u c", u=8)
    lob = ATa[:, 24576:26624]
    hib = ATa[:, 26624:28672]
    yh = ATa[:, 28672:30720]
    xnt_stage = ATa[:, 30720:32768]

    zcb = SCRb[:, 0:1024]
    gcb = SCRb[:, 1024:2048]
    zgb = SCRb[:, 2048:3072]
    qlo = SCR8[:, 0:2048]
    qhi = SCR8[:, 2048:4096]
    msk = SCR[:, 4096:6144]

    xs = (F32A[:, 0, :], F32A[:, 1, :])
    txg = F32A[:, 2, :]
    x1s = (F32A[:, 0, :], F32A[:, 1, :], F32A[:, 2, :])
    dtrep_s = XnT.bitcast(F32).rearrange(
        "p a b -> p (a b)")[:, 0:4096]

    def f(ti, j):
        return st[:, ti, j:j + 1]

    e.phase("S")
    e.op("scalar", lambda E: E.dma_start(dtc[:, :], t["dtcol"][:, :]), "stp", k=16)
    e.op("scalar", lambda E: E.dma_start(scb[:, :], t["scol"][:, :]), "stp", k=16)
    e.op("scalar", lambda E: E.dma_start(cst[:, 3:4], wscol[:, :]), "stp", "ld_all", k=16)
    e.wait("vector", "stp", "ld_all")
    e.op("vector", lambda E: E.tensor_copy(cst[:, 2:3], scb[:, :]))
    e.op("vector", lambda E: E.drain())
    e.op("vector", lambda E: E.reciprocal(cst[:, 0:1], cst[:, 2:3]))
    e.op("vector", lambda E: E.drain())
    e.op("vector", lambda E: E.tensor_tensor(cst[:, 1:2], cst[:, 2:3], cst[:, 3:4], ALU.mult),
         "dve", "consts")
    e.op("gpsimd", lambda E: E.dma_start(dtrep_s[:, :], dtrep[:, :]), "cast", "ld_dt", k=16)
    e.wait("vector", "cast", "ld_dt")
    e.op("vector", lambda E: E.tensor_scalar(dtrep_s[:, :], dtrep_s[:, :], -1.0, None, ALU.mult))
    e.op("vector", lambda E: E.drain())
    e.op("vector", lambda E: E.reduce_max(sm[:, 0:1], dtrep_s[:, :], axis=AX.X))
    e.op("vector", lambda E: E.drain())
    e.op("vector", lambda E: E.tensor_scalar(sm[:, 1:2], sm[:, 0:1], -1.0, None, ALU.mult),
         "dve", "negm")
    e.wait("scalar", "dve", "negm")
    e.op("scalar", lambda E: E.activation(dtrep_s[:, :], dtrep_s[:, :], AF.Exp,
                                          bias=sm[:, 1:2]), "act", "expden")
    e.wait("vector", "act", "expden")
    e.op("vector", lambda E: E.reduce_sum(sm[:, 2:3], dtrep_s[:, :], axis=AX.X))
    e.op("vector", lambda E: E.drain())
    e.op("vector", lambda E: E.reciprocal(sm[:, 3:4], sm[:, 2:3]), "dve", "rs")
    e.wait("scalar", "stp", "ld_all")
    e.op("scalar", lambda E: E.activation(dtw[:, :], dtc[:, :], AF.Exp,
                                          scale=-1.0, bias=sm[:, 1:2]), "act", "expcol")
    e.wait("vector", "act", "expcol")
    e.op("vector", lambda E: E.drain())
    e.op("vector", lambda E: E.tensor_scalar(dtw[:, :], dtw[:, :], sm[:, 3:4], None,
                                             ALU.mult), "dve", "dtw")
    e.op("gpsimd", lambda E: E.dma_start(gbf[:, :], grep[:, :]), "cast", "cgb", k=16)
    e.op("gpsimd", lambda E: E.dma_start(bbf[:, :], brep[:, :]), "cast", "cbb", k=16)
    def emit_cast_zg(g):
        e.op("gpsimd", lambda E, g=g: E.dma_start(
            winb[512 * g:512 * (g + 1), :], win[512 * g:512 * (g + 1), :]),
            "cast", f"cz{g}", k=16)
        e.op("gpsimd", lambda E, g=g: E.dma_start(
            winb[4096 + 512 * g:4096 + 512 * (g + 1), :],
            win[4096 + 512 * g:4096 + 512 * (g + 1), :]), "cast", f"cg{g}", k=16)

    cast_steps = []
    for g in range(1, NG):
        cast_steps.append(("z", g))
        cast_steps.append(("g", g))
    for c in range(NDC):
        cast_steps.append(("w", c))
    for c in range(2):
        cast_steps.append(("3", c))

    def emit_cast_step(step):
        kind, idx = step
        if kind == "z":
            e.op("gpsimd", lambda E, g=idx: E.dma_start(
                winb[512 * g:512 * (g + 1), :], win[512 * g:512 * (g + 1), :]),
                "cast", f"cz{idx}", k=16)
        elif kind == "g":
            e.op("gpsimd", lambda E, g=idx: E.dma_start(
                winb[4096 + 512 * g:4096 + 512 * (g + 1), :],
                win[4096 + 512 * g:4096 + 512 * (g + 1), :]),
                "cast", f"cg{idx}", k=16)
        elif kind == "w":
            e.op("gpsimd", lambda E, c=idx: E.dma_start(
                woutb[512 * c:512 * (c + 1), :], wout[512 * c:512 * (c + 1), :]),
                "cast", f"cw{idx}", k=16)
        else:
            e.op("gpsimd", lambda E, c=idx: E.dma_start(
                w3b[1024 * c:1024 * (c + 1), :], w8[1024 * c:1024 * (c + 1), :]),
                "cast", f"c3{idx}", k=16)

    emit_cast_zg(0)

    for h in range(HALVES):
        hb = h * NT
        r0 = h * HT

        e.phase(f"A{h}")
        for i in range(NT):
            ti = hb + i
            if h == 1 and i < 2:
                e.wait("scalar", "out0", value=e.at(f"outw{0}_{6}"))
                e.wait("scalar", "out1", value=e.at(f"outw{0}_{7}"))
                e.wait("scalar", "out2", value=e.at(f"outw{0}_{5}"))
            if i >= 2:
                e.wait("scalar", "act", f"hAa{h}_{i-2}")
                e.wait("scalar", "dve", f"hAd{h}_{i-2}")
            e.op("scalar", lambda E, i=i, r0=r0: E.dma_start(
                xs[i % 2][:, :], x[r0 + i * 128:r0 + (i + 1) * 128, :]),
                f"ld{i % 2}", f"ldxA{h}_{i}", k=16)
            e.wait("scalar", f"ld{i % 2}", f"ldxA{h}_{i}")
            e.op("scalar", lambda E, i=i, ti=ti: E.activation(
                txg[:, :], xs[i % 2][:, :], AF.Square, accum_out=f(ti, 2)),
                "act", f"hAa{h}_{i}")
            e.wait("vector", f"ld{i % 2}", f"ldxA{h}_{i}")
            e.op("vector", lambda E, i=i, ti=ti: E.reduce_sum(
                f(ti, 0), xs[i % 2][:, :], axis=AX.X), "dve", f"hAd{h}_{i}")
            e.op("vector", lambda E: E.drain())

        def bv(fn, sem=None, mark=None, hb=hb):
            for q in range(NT):
                last = q == NT - 1
                e.op("vector", (lambda E, q=q, hb=hb: fn(E, hb + q)),
                     sem if last else None, mark if last else None)
            e.op("vector", lambda E: E.drain())

        def ba(fn, sem=None, mark=None, hb=hb):
            for q in range(NT):
                last = q == NT - 1
                e.op("scalar", (lambda E, q=q, hb=hb: fn(E, hb + q)),
                     sem if last else None, mark if last else None)

        e.wait("vector", "act", f"hAa{h}_{NT-1}")
        bv(lambda E, ti: E.tensor_scalar(f(ti, 1), f(ti, 0), RD, None, ALU.mult))
        bv(lambda E, ti: E.tensor_scalar(f(ti, 3), f(ti, 2), RD, None, ALU.mult))
        bv(lambda E, ti: E.tensor_tensor(f(ti, 4), f(ti, 1), f(ti, 1), ALU.mult))
        bv(lambda E, ti: E.tensor_tensor(f(ti, 5), f(ti, 3), f(ti, 4), ALU.subtract))
        bv(lambda E, ti: E.tensor_scalar(f(ti, 6), f(ti, 5), EPS, None, ALU.add),
           "dve", f"bA1{h}")
        e.wait("scalar", "dve", f"bA1{h}")
        ba(lambda E, ti: E.activation(f(ti, 7), f(ti, 6), AF.Sqrt), "act", f"bA1s{h}")
        e.wait("vector", "act", f"bA1s{h}")
        bv(lambda E, ti: E.reciprocal(f(ti, 8), f(ti, 7)))
        bv(lambda E, ti: E.tensor_tensor(f(ti, 4), f(ti, 8), f(ti, 8), ALU.mult))
        bv(lambda E, ti: E.tensor_tensor(f(ti, 4), f(ti, 4), f(ti, 6), ALU.mult))
        bv(lambda E, ti: E.tensor_scalar(f(ti, 4), f(ti, 4), -0.5, 1.5, ALU.mult, ALU.add))
        bv(lambda E, ti: E.tensor_tensor(f(ti, 8), f(ti, 8), f(ti, 4), ALU.mult))
        bv(lambda E, ti: E.tensor_scalar(f(ti, 9), f(ti, 8), 1.0, None, ALU.add))
        bv(lambda E, ti: E.scalar_tensor_tensor(
            f(ti, 10), f(ti, 1), -1.0, f(ti, 8), ALU.mult, ALU.mult))
        bv(lambda E, ti: E.tensor_tensor(f(ti, 11), f(ti, 8), f(ti, 8), ALU.mult))
        bv(lambda E, ti: E.tensor_tensor(f(ti, 11), f(ti, 11), f(ti, 5), ALU.mult))
        bv(lambda E, ti: E.tensor_scalar(f(ti, 11), f(ti, 11), EPS, None, ALU.add),
           "dve", f"bA2{h}")
        e.wait("scalar", "dve", f"bA2{h}")
        ba(lambda E, ti: E.activation(f(ti, 12), f(ti, 11), AF.Sqrt), "act", f"bA2s{h}")
        e.wait("vector", "act", f"bA2s{h}")
        bv(lambda E, ti: E.reciprocal(f(ti, 13), f(ti, 12)))
        bv(lambda E, ti: E.tensor_tensor(f(ti, 4), f(ti, 13), f(ti, 13), ALU.mult))
        bv(lambda E, ti: E.tensor_tensor(f(ti, 4), f(ti, 4), f(ti, 11), ALU.mult))
        bv(lambda E, ti: E.tensor_scalar(f(ti, 4), f(ti, 4), -0.5, 1.5, ALU.mult, ALU.add))
        bv(lambda E, ti: E.tensor_tensor(f(ti, 13), f(ti, 13), f(ti, 4), ALU.mult))
        bv(lambda E, ti: E.tensor_tensor(f(ti, 14), f(ti, 8), f(ti, 13), ALU.mult))
        bv(lambda E, ti: E.tensor_tensor(f(ti, 15), f(ti, 10), f(ti, 13), ALU.mult),
           "dve", f"bA{h}")

        tg = (txg, ybuf)
        xst = (xnt_stage, yh)
        for i in range(NT):
            ti = hb + i
            if i < 2:
                e.wait("scalar", "dve", f"bA{h}")
            else:
                e.wait("scalar", "act", f"tA{h}_{i-2}")
            e.op("scalar", lambda E, i=i, r0=r0: E.dma_start(
                xs[i % 2][:, :], x[r0 + i * 128:r0 + (i + 1) * 128, :]),
                f"ld{i % 2}", f"ldA2{h}_{i}", k=16)
            e.wait("scalar", f"ld{i % 2}", f"ldA2{h}_{i}")
            if i >= 2:
                e.wait("scalar", "gp", f"xnA{h}_{i-2}")
            e.op("scalar", lambda E, i=i, ti=ti: E.activation(
                tg[i % 2][:, :], xs[i % 2][:, :], AF.Identity,
                scale=f(ti, 14), bias=f(ti, 15)), "act", f"tA{h}_{i}")
            e.wait("vector", "act", f"tA{h}_{i}")
            if h == 0 and i == 0:
                e.wait("vector", "cast", "cgb")
            e.op("vector", lambda E, i=i: E.tensor_tensor(
                tg[i % 2][:, :], tg[i % 2][:, :], gbf[:, :], ALU.mult),
                "dve", f"xgA{h}_{i}")
            e.wait("gpsimd", "dve", f"xgA{h}_{i}")
            if h == 0 and i == 0:
                e.wait("gpsimd", "cast", "cbb")
            if i >= 2:
                e.wait("gpsimd", "xnw", f"xnw{h}_{i-2}")
            if h == 1 and i == 0:
                e.wait("gpsimd", "pe", f"peC{0}_{15}")
            e.op("gpsimd", lambda E, i=i: E.tensor_tensor(
                xst[i % 2][:, :], tg[i % 2][:, :], bbf[:, :], ALU.add),
                "gp", f"xnA{h}_{i}")
            e.wait("sync", "gp", f"xnA{h}_{i}")
            e.op("sync", lambda E, i=i: E.dma_start(
                xnb[i * 128:(i + 1) * 128, :], xst[i % 2][:, :]),
                "xnw", f"xnw{h}_{i}", k=16)
        e.wait("sync", "xnw", f"xnw{h}_{7}")
        e.wait("scalar", "xnw", f"xnw{h}_{7}")
        if h == 0:
            e.wait("sync", "dve", "rs")
            e.wait("scalar", "dve", "rs")
        if h == 1:
            e.wait("sync", "pe", f"peB{0}_{NP-1}")
            e.wait("scalar", "pe", f"peB{0}_{NP-1}")
        for dp in range(8):
            e.op("sync", lambda E, dp=dp: E.dma_start(
                XnT[:, dp, :], xnb[0:HT, dp * 128:(dp + 1) * 128], transpose=True),
                "xnt", f"xnt{h}_{dp}", k=16)
        for dp in range(8, 16):
            e.op("scalar", lambda E, dp=dp: E.dma_start(
                XnT[:, dp, :], xnb[0:HT, dp * 128:(dp + 1) * 128], transpose=True),
                "xnt", f"xnt{h}_{dp}", k=16)

        e.phase(f"B{h}")
        for g in range(NG):
            par = g % 2
            if h == 0:
                e.wait("sync", "cast", f"cz{g}")
                e.wait("sync", "cast", f"cg{g}")
            if h == 1 and g < 2:
                e.wait("sync", "pe", f"peD{0}_{7}")
            if g >= 2:
                e.wait("sync", "pe", f"peB{h}_{4*(g-2)+3}")
            for dp in range(16):
                e.op("sync", lambda E, g=g, dp=dp, par=par: E.dma_start(
                    zs(par)[:, dp, :],
                    winb[512 * g:512 * (g + 1), dp * 128:(dp + 1) * 128],
                    transpose=True), "wzA" if par == 0 else "wzB", k=16)
            for dp in range(16):
                e.op("sync", lambda E, g=g, dp=dp, par=par: E.dma_start(
                    gs(par)[:, dp, :],
                    winb[4096 + 512 * g:4096 + 512 * (g + 1), dp * 128:(dp + 1) * 128],
                    transpose=True), "wzA" if par == 0 else "wzB",
                    f"wz{h}_{g}" if dp == 15 else None, k=16)

        for p in range(NP):
            g = p // 4
            par = g % 2
            sl = (p % 4) * 128
            if p % 4 == 0:
                e.wait("tensor", "wzA" if par == 0 else "wzB", f"wz{h}_{g}")
            if p == 0:
                e.wait("tensor", "xnt", f"xnt{h}_{15}")
                e.wait("tensor", "dve", "dtw")
            pf = t["psum_free"].get((h, "B", p))
            if pf:
                e.wait("tensor", pf[0], value=pf[1])
            for k in range(16):
                for tc in range(2):
                    e.op("tensor", lambda E, p=p, k=k, tc=tc, par=par, sl=sl: E.matmul(
                        P[p % 2][:, tc * 512:(tc + 1) * 512],
                        zs(par)[:, k, sl:sl + 128],
                        XnT[:, k, tc * 512:(tc + 1) * 512],
                        start=(k == 0), stop=(k == 15)))
            for k in range(16):
                for tc in range(2):
                    e.op("tensor", lambda E, p=p, k=k, tc=tc, par=par, sl=sl: E.matmul(
                        P[p % 2][:, 1024 + tc * 512:1024 + (tc + 1) * 512],
                        gs(par)[:, k, sl:sl + 128],
                        XnT[:, k, tc * 512:(tc + 1) * 512],
                        start=(k == 0), stop=(k == 15)),
                        sem="pe" if (k == 15 and tc == 1) else None,
                        mark=f"peB{h}_{p}" if (k == 15 and tc == 1) else None)

            e.wait("vector", "pe", f"peB{h}_{p}")
            if p >= 1:
                e.wait("vector", "gp", f"mulB{h}_{p-1}")
            e.op("vector", lambda E, p=p: E.tensor_scalar(
                zcb[:, :], P[p % 2][:, 0:1024], -15.0, 15.0, ALU.max, ALU.min),
                "dve", f"czB{h}_{p}")
            e.op("vector", lambda E, p=p: E.tensor_scalar(
                gcb[:, :], P[p % 2][:, 1024:2048], -15.0, 15.0, ALU.max, ALU.min),
                "dve", f"cgB{h}_{p}")
            e.wait("vector", "act", f"sigB{h}_{p}")
            e.op("vector", lambda E, p=p: E.tensor_tensor(
                zgb[:, :], P[p % 2][:, 0:1024], zcb[:, :], ALU.mult),
                "dve", f"zgB{h}_{p}")
            e.op("vector", lambda E: E.drain())
            e.wait("scalar", "dve", f"czB{h}_{p}")
            e.op("scalar", lambda E: E.activation(zcb[:, :], zcb[:, :], AF.Sigmoid),
                 "act", f"sigB{h}_{p}")
            e.wait("scalar", "dve", f"cgB{h}_{p}")
            e.wait("scalar", "dve", f"zgB{h}_{p}")
            e.op("scalar", lambda E: E.activation(zcb[:, :], gcb[:, :], AF.Sigmoid),
                 "act", f"sigG{h}_{p}")
            e.wait("gpsimd", "act", f"sigG{h}_{p}")
            if h == 1 and p == 0:
                e.wait("gpsimd", "dve", f"selD{0}_{7}")
            e.op("gpsimd", lambda E: E.tensor_tensor(
                gcb[:, :], gcb[:, :], zcb[:, :], ALU.mult), "gp", f"mulB{h}_{p}")
            if h == 0 and p < len(cast_steps):
                emit_cast_step(cast_steps[p])
            e.wait("vector", "gp", f"mulB{h}_{p}")
            if h == 1 and p == 0:
                e.wait("vector", "xnw", f"xnw{1}_{7}")
                e.wait("vector", "pe", f"peD{0}_{7}")
                e.wait("vector", "yhwB", f"yhw{0}_{7}")
            if h == 0 and p == 0:
                e.wait("vector", "dve", "dtw")
            e.op("vector", lambda E, p=p: E.scalar_tensor_tensor(
                ATl(p)[:, :], zgb[:, :], dtw[:, p:p + 1], gcb[:, :],
                ALU.mult, ALU.mult), "dve", f"sttB{h}_{p}")

        e.phase(f"C{h}")
        def emit_wo(c):
            par = c % 2
            q = "sync" if par == 0 else "scalar"
            if h == 0:
                e.wait(q, "cast", f"cw{c}")
            if c < 2:
                e.wait(q, "pe", f"peB{h}_{27 if c == 0 else 31}")
            else:
                e.wait(q, "pe", f"peC{h}_{4*(c-2)+3}")
            for ep in range(32):
                e.op(q, lambda E, c=c, ep=ep, par=par: E.dma_start(
                    wo(par)[:, ep, :],
                    woutb[512 * c:512 * (c + 1), ep * 128:(ep + 1) * 128],
                    transpose=True), "woA" if par == 0 else "woB",
                    f"wo{h}_{c}" if ep == 31 else None, k=16)

        emit_wo(0)
        emit_wo(1)
        emit_wo(2)
        for dt_ in range(ND):
            if dt_ == 8:
                emit_wo(3)
            c = dt_ // 4
            par = c % 2
            sl = (dt_ % 4) * 128
            if dt_ % 4 == 0:
                e.wait("tensor", "woA" if par == 0 else "woB", f"wo{h}_{c}")
            if dt_ == 0:
                e.wait("tensor", "dve", f"sttB{h}_{NP-1}")
            pf = t["psum_free"].get((h, "C", dt_))
            if pf:
                e.wait("tensor", pf[0], value=pf[1])
            for ep in range(32):
                for tc in range(2):
                    e.op("tensor", lambda E, dt_=dt_, ep=ep, tc=tc, par=par, sl=sl: E.matmul(
                        P[dt_ % 2][:, tc * 512:(tc + 1) * 512],
                        wo(par)[:, ep, sl:sl + 128],
                        ATl(ep)[:, tc * 512:(tc + 1) * 512],
                        start=(ep == 0), stop=(ep == 31)),
                        sem="pe" if (ep == 31 and tc == 1) else None,
                        mark=f"peC{h}_{dt_}" if (ep == 31 and tc == 1) else None)
            e.wait("vector", "pe", f"peC{h}_{dt_}")
            if dt_ >= 2:
                e.wait("vector", "o1wA" if dt_ % 2 == 0 else "o1wB", f"o1w{h}_{dt_-2}")
            e.op("vector", lambda E, dt_=dt_: E.tensor_copy(
                o1e[:, (dt_ % 2) * 1024:(dt_ % 2 + 1) * 1024], P[dt_ % 2][:, 0:1024]),
                "dve", f"o1eC{h}_{dt_}")
            e.wait("scalar", "dve", f"o1eC{h}_{dt_}")
            e.op("scalar", lambda E, dt_=dt_: E.dma_start(
                o1b[dt_ * 128:(dt_ + 1) * 128, :],
                o1e[:, (dt_ % 2) * 1024:(dt_ % 2 + 1) * 1024]),
                "o1wA" if dt_ % 2 == 0 else "o1wB", f"o1w{h}_{dt_}", k=16)

        e.phase(f"D{h}")
        e.wait("sync", "pe", f"peC{h}_{15}")
        e.wait("sync", "o1wA", f"o1w{h}_{14}")
        e.wait("sync", "o1wB", f"o1w{h}_{15}")
        e.wait("scalar", "o1wA", f"o1w{h}_{14}")
        e.wait("scalar", "o1wB", f"o1w{h}_{15}")
        for i in range(0, NT, 2):
            e.op("sync", lambda E, i=i: E.dma_start(
                o1t[:, i, :], o1b[0:2048, i * 128:(i + 1) * 128], transpose=True),
                f"ot{i}", f"o1t{h}_{i}", k=16)
        for i in range(1, NT, 2):
            e.op("scalar", lambda E, i=i: E.dma_start(
                o1t[:, i, :], o1b[0:2048, i * 128:(i + 1) * 128], transpose=True),
                f"ot{i}", f"o1t{h}_{i}", k=16)
        if h == 0:
            e.wait("sync", "cast", "c31")
        for k in range(16):
            e.op("sync", lambda E, k=k: E.dma_start(
                w3T[:, k, :], w3b[0:2048, k * 128:(k + 1) * 128], transpose=True),
                "w3t", f"w3t{h}_{k}" if k == 15 else None, k=16)

        for i in range(NT):
            ti = hb + i
            if i == 0:
                e.wait("scalar", "act", f"tA{h}_{NT-2}")
            elif i == 1:
                e.wait("scalar", "act", f"tA{h}_{NT-1}")
            else:
                e.wait("scalar", "dve", f"hDd{h}_{i-2}")
                e.wait("scalar", "act", f"hDa{h}_{i-2}")
            e.op("scalar", lambda E, i=i, r0=r0: E.dma_start(
                xs[i % 2][:, :], x[r0 + i * 128:r0 + (i + 1) * 128, :]),
                f"ld{i % 2}", f"ldxD{h}_{i}", k=16)
            e.wait("scalar", f"ld{i % 2}", f"ldxD{h}_{i}")
            e.op("scalar", lambda E, i=i, ti=ti: E.activation(
                xs[i % 2][:, :], xs[i % 2][:, :], AF.Identity,
                scale=f(ti, 9), bias=f(ti, 10)), "act", f"xaD{h}_{i}")
            if i >= 1:
                e.wait("scalar", "dve", f"x1D{h}_{i-1}")
                e.op("scalar", lambda E, i=i, ti=ti: E.activation(
                    ybuf[:, :], xs[(i - 1) % 2][:, :], AF.Square,
                    accum_out=f(ti - 1, 16)), "act", f"hDa{h}_{i-1}")
            e.wait("vector", "act", f"xaD{h}_{i}")
            e.wait("vector", f"ot{i}", f"o1t{h}_{i}")
            e.op("vector", lambda E, i=i: E.tensor_tensor(
                xs[i % 2][:, :], xs[i % 2][:, :], o1t[:, i, :], ALU.add),
                "dve", f"x1D{h}_{i}")
            e.op("vector", lambda E: E.drain())
            e.op("vector", lambda E, i=i, ti=ti: E.reduce_sum(
                f(ti, 17), xs[i % 2][:, :], axis=AX.X), "dve", f"hDd{h}_{i}")
            e.op("vector", lambda E: E.drain())
        e.wait("scalar", "dve", f"x1D{h}_{NT-1}")
        e.op("scalar", lambda E, ti=hb + NT - 1: E.activation(
            ybuf[:, :], xs[(NT - 1) % 2][:, :], AF.Square,
            accum_out=f(ti, 16)), "act", f"hDa{h}_{NT-1}")

        def bv(fn, sem=None, mark=None, hb=hb):
            for q in range(NT):
                last = q == NT - 1
                e.op("vector", (lambda E, q=q, hb=hb: fn(E, hb + q)),
                     sem if last else None, mark if last else None)
            e.op("vector", lambda E: E.drain())

        def ba(fn, sem=None, mark=None, hb=hb):
            for q in range(NT):
                last = q == NT - 1
                e.op("scalar", (lambda E, q=q, hb=hb: fn(E, hb + q)),
                     sem if last else None, mark if last else None)

        e.wait("vector", "act", f"hDa{h}_{NT-1}")
        if h == 0:
            e.wait("vector", "dve", "consts")
        bv(lambda E, ti: E.tensor_scalar(f(ti, 18), f(ti, 17), RD, None, ALU.mult))
        bv(lambda E, ti: E.tensor_scalar(f(ti, 19), f(ti, 16), RD, None, ALU.mult))
        bv(lambda E, ti: E.tensor_tensor(f(ti, 4), f(ti, 18), f(ti, 18), ALU.mult))
        bv(lambda E, ti: E.tensor_tensor(f(ti, 19), f(ti, 19), f(ti, 4), ALU.subtract))
        bv(lambda E, ti: E.tensor_scalar(f(ti, 19), f(ti, 19), EPS, None, ALU.add),
           "dve", f"bD1{h}")
        e.wait("scalar", "dve", f"bD1{h}")
        ba(lambda E, ti: E.activation(f(ti, 5), f(ti, 19), AF.Sqrt), "act", f"bD1s{h}")
        e.wait("vector", "act", f"bD1s{h}")
        bv(lambda E, ti: E.reciprocal(f(ti, 6), f(ti, 5)))
        bv(lambda E, ti: E.tensor_tensor(f(ti, 4), f(ti, 6), f(ti, 6), ALU.mult))
        bv(lambda E, ti: E.tensor_tensor(f(ti, 4), f(ti, 4), f(ti, 19), ALU.mult))
        bv(lambda E, ti: E.tensor_scalar(f(ti, 4), f(ti, 4), -0.5, 1.5, ALU.mult, ALU.add))
        bv(lambda E, ti: E.tensor_tensor(f(ti, 6), f(ti, 6), f(ti, 4), ALU.mult))
        bv(lambda E, ti: E.tensor_scalar(f(ti, 7), f(ti, 6), cst[:, 0:1], None, ALU.mult))
        bv(lambda E, ti: E.scalar_tensor_tensor(
            f(ti, 8), f(ti, 18), -1.0, f(ti, 7), ALU.mult, ALU.mult), "dve", f"bD{h}")

        def emit_out(i):
            e.wait("vector", "pe", f"peD{h}_{i}")
            e.op("vector", lambda E, i=i: E.scalar_tensor_tensor(
                x1s[i % 3][:, :], P[i % 2][:, :], cst[:, 1:2], x1s[i % 3][:, :],
                ALU.mult, ALU.add), "dve", f"outD{h}_{i}")
            e.wait("scalar", "dve", f"outD{h}_{i}")
            e.op("scalar", lambda E, i=i, r0=r0: E.dma_start(
                out[r0 + i * 128:r0 + (i + 1) * 128, :], x1s[i % 3][:, :]),
                f"out{i % 3}", f"outw{h}_{i}", k=16)

        XnTb = XnT.rearrange("p a b -> p (a b)")
        XnT8 = XnT.bitcast(F8)
        XnTu = XnT.bitcast(U8)
        ybs = (ybuf, XnTb[:, 4 * 1024:8 * 1024].bitcast(F32))
        lobs = (lob, XnTb[:, 0:2048])
        hibs = (hib, XnTb[:, 2048:4096])
        yhs = (yh, XnTb[:, 12 * 1024:14 * 1024])
        msks = (msk, XnTu[:, 8, :])
        qlos = (qlo, XnT8[:, 9, :])
        qhis = (qhi, XnT8[:, 10, :])

        def qdq(j):
            q = j % 2
            u = j // 2
            e.wait("scalar", "dve", f"x1b{h}_{j}")
            if j >= 2:
                e.wait("scalar", "dve", f"upD{h}_{j-2}")
                e.wait("scalar", "dve", f"selD{h}_{j-2}")
            e.op("scalar", lambda E, j=j, ti=hb + j, q=q: E.activation(
                ybs[q][:, :], x1s[j % 3][:, :], AF.Identity,
                scale=f(ti, 7), bias=f(ti, 8)), "act", f"yD{h}_{j}")
            e.wait("vector", "act", f"yD{h}_{j}")
            e.op("vector", lambda E, q=q: E.tensor_scalar(
                ybs[q][:, :], ybs[q][:, :], -448.0, 448.0, ALU.max, ALU.min),
                "dve", f"vcD{h}_{j}")
            e.wait("scalar", "dve", f"vcD{h}_{j}")
            e.op("scalar", lambda E, q=q: E.activation(
                qlos[q][:, :], ybs[q][:, :], AF.Copy))
            e.op("scalar", lambda E, q=q: E.activation(
                qhis[q][:, :], ybs[q][:, :], AF.Copy, scale=0.5))
            e.op("scalar", lambda E: E.drain())
            e.op("scalar", lambda E, q=q: E.activation(
                msks[q][:, :], qlos[q][:, :], AF.Is_finite), "act", f"m8D{h}_{j}")
            e.wait("vector", "act", f"m8D{h}_{j}")
            e.op("vector", lambda E, q=q: E.tensor_copy(lobs[q][:, :], qlos[q][:, :]))
            e.op("vector", lambda E, q=q: E.tensor_scalar(
                hibs[q][:, :], qhis[q][:, :], 2.0, None, ALU.mult), "dve", f"upD{h}_{j}")
            e.op("vector", lambda E: E.drain())
            if j >= 2:
                e.wait("vector", "yhwA" if ((j - 2) // 2) % 2 == 0 else "yhwB",
                       f"yhw{h}_{j-2}")
            e.op("vector", lambda E, q=q: E.select(
                yhs[q][:, :], msks[q][:, :], lobs[q][:, :], hibs[q][:, :]),
                "dve", f"selD{h}_{j}")
            e.wait("scalar", "dve", f"selD{h}_{j}")
            e.op("scalar", lambda E, j=j, q=q: E.dma_start(
                yhb[j * 128:(j + 1) * 128, :], yhs[q][:, :]),
                "yhwA" if u % 2 == 0 else "yhwB", f"yhw{h}_{j}", k=16)
            e.wait("sync", "yhwA" if u % 2 == 0 else "yhwB", f"yhw{h}_{j}")
            if j % 2 == 0 and u >= 2:
                e.wait("sync", "pe", f"peD{h}_{2*(u-2)+1}")
            for k in range(16):
                e.op("sync", lambda E, j=j, u=u, k=k: E.dma_start(
                    yhT[:, u % 2, k, (j % 2) * 128:(j % 2) * 128 + 128],
                    yhb[j * 128:(j + 1) * 128, k * 128:(k + 1) * 128],
                    transpose=True), "yhtA" if u % 2 == 0 else "yhtB",
                    f"yht{h}_{j}" if k == 15 else None, k=16)
            e.wait("tensor", "yhtA" if u % 2 == 0 else "yhtB", f"yht{h}_{j}")
            if j == 0:
                e.wait("tensor", "w3t", f"w3t{h}_{15}")
            pf = t["psum_free"].get((h, "D", j))
            if pf:
                e.wait("tensor", pf[0], value=pf[1])
            for k in range(16):
                for oc in range(4):
                    e.op("tensor", lambda E, j=j, u=u, k=k, oc=oc: E.matmul(
                        P[j % 2][:, oc * 512:(oc + 1) * 512],
                        yhT[:, u % 2, k, (j % 2) * 128:(j % 2) * 128 + 128],
                        w3T[:, k, oc * 512:(oc + 1) * 512],
                        start=(k == 0), stop=(k == 15)),
                        sem="pe" if (k == 15 and oc == 3) else None,
                        mark=f"peD{h}_{j}" if (k == 15 and oc == 3) else None)

        for i in range(NT):
            ti = hb + i
            if i < 3:
                e.wait("scalar", "dve", f"bD{h}")
            else:
                e.wait("scalar", f"out{(i - 3) % 3}", f"outw{h}_{i-3}")
            e.op("scalar", lambda E, i=i, r0=r0: E.dma_start(
                x1s[i % 3][:, :], x[r0 + i * 128:r0 + (i + 1) * 128, :]),
                f"ld{i % 3}", f"ldD2{h}_{i}", k=16)
            if i >= 1:
                qdq(i - 1)
            e.wait("scalar", f"ld{i % 3}", f"ldD2{h}_{i}")
            e.op("scalar", lambda E, i=i, ti=ti: E.activation(
                x1s[i % 3][:, :], x1s[i % 3][:, :], AF.Identity,
                scale=f(ti, 9), bias=f(ti, 10)), "act", f"xbD{h}_{i}")
            e.wait("vector", "act", f"xbD{h}_{i}")
            e.wait("vector", f"ot{i}", f"o1t{h}_{i}")
            e.op("vector", lambda E, i=i: E.tensor_tensor(
                x1s[i % 3][:, :], x1s[i % 3][:, :], o1t[:, i, :], ALU.add),
                "dve", f"x1b{h}_{i}")
            e.op("vector", lambda E: E.drain())
            if i >= 2:
                emit_out(i - 2)
        qdq(NT - 1)
        emit_out(NT - 2)
        emit_out(NT - 1)

    e.phase("F")
    e.wait("sync", "out0", f"outw{HALVES-1}_{6}")
    e.wait("sync", "out1", f"outw{HALVES-1}_{7}")
    e.wait("sync", "out2", f"outw{HALVES-1}_{5}")


def compute_psum_free(marks):
    freed = {}
    order = []
    for h in range(HALVES):
        for p in range(NP):
            order.append(((h, "B", p), p % 2, ("dve", f"zgB{h}_{p}")))
        for d_ in range(ND):
            order.append(((h, "C", d_), d_ % 2, ("dve", f"o1eC{h}_{d_}")))
        for i in range(NT):
            order.append(((h, "D", i), i % 2, ("dve", f"outD{h}_{i}")))
    last = {}
    for key, par, freemark in order:
        if par in last:
            sem, mk = last[par]
            freed[key] = (sem, marks.get(mk, 0))
        last[par] = freemark
    return freed


def build_program():
    nc = bass.Bass("TRN2", target_bir_lowering=False)
    t = {}
    t["x"] = nc.dram_tensor("x", [TCORE, D], F32, kind="ExternalInput")
    t["grep"] = nc.dram_tensor("grep", [128, D], F32, kind="ExternalInput")
    t["brep"] = nc.dram_tensor("brep", [128, D], F32, kind="ExternalInput")
    t["win"] = nc.dram_tensor("win", [2 * E, D], F32, kind="ExternalInput")
    t["wout"] = nc.dram_tensor("wout", [D, E], F32, kind="ExternalInput")
    t["w8"] = nc.dram_tensor("w8", [D, D], F8, kind="ExternalInput")
    t["dtrep"] = nc.dram_tensor("dtrep", [128, E], F32, kind="ExternalInput")
    t["dtcol"] = nc.dram_tensor("dtcol", [128, 32], F32, kind="ExternalInput")
    t["scol"] = nc.dram_tensor("scol", [128, 1], BF16, kind="ExternalInput")
    t["wscol"] = nc.dram_tensor("wscol", [128, 1], F32, kind="ExternalInput")
    t["out"] = nc.dram_tensor("out", [TCORE, D], F32, kind="ExternalOutput")
    t["winb"] = nc.dram_tensor("winb", [2 * E, D], BF16, kind="Internal")
    t["woutb"] = nc.dram_tensor("woutb", [D, E], BF16, kind="Internal")
    t["w3b"] = nc.dram_tensor("w3b", [D, D], BF16, kind="Internal")
    t["xnb"] = nc.dram_tensor("xnb", [HT, D], BF16, kind="Internal")
    t["o1b"] = nc.dram_tensor("o1b", [D, HT], BF16, kind="Internal")
    t["yhb"] = nc.dram_tensor("yhb", [HT, D], BF16, kind="Internal")

    with ExitStack() as ctx:
        en = ctx.enter_context
        t["st"] = en(nc.sbuf_tensor("st", [128, 16, 20], F32))
        t["dtw"] = en(nc.sbuf_tensor("dtw", [128, 32], F32))
        t["dtc"] = en(nc.sbuf_tensor("dtc", [128, 32], F32))
        t["cst"] = en(nc.sbuf_tensor("cst", [128, 4], F32))
        t["sm"] = en(nc.sbuf_tensor("sm", [128, 4], F32))
        t["scb"] = en(nc.sbuf_tensor("scb", [128, 1], BF16))
        t["gbf"] = en(nc.sbuf_tensor("gbf", [128, D], BF16))
        t["bbf"] = en(nc.sbuf_tensor("bbf", [128, D], BF16))
        t["XnT"] = en(nc.sbuf_tensor("XnT", [128, 16, HT], BF16))
        t["ATa"] = en(nc.sbuf_tensor("ATa", [128, 32768], BF16))
        t["WAa"] = en(nc.sbuf_tensor("WAa", [128, 32768], BF16))
        t["F32A"] = en(nc.sbuf_tensor("F32A", [128, 3, D], F32))
        t["ybuf"] = en(nc.sbuf_tensor("ybuf", [128, D], F32))
        t["SCR"] = en(nc.sbuf_tensor("SCR", [128, 6144], U8))
        t["P0"] = en(nc.psum_tensor("P0", [128, D], F32))
        t["P1"] = en(nc.psum_tensor("P1", [128, D], F32))

        sem_names = ["stp", "cast", "wzA", "wzB", "gsA", "gsB", "woA", "woB",
                     "w3t", "xnw", "xnt",
                     "o1wA", "o1wB", "ot0", "ot1", "ot2", "ot3", "ot4", "ot5",
                     "ot6", "ot7", "yhwA", "yhwB", "yhtA", "yhtB",
                     "ld0", "ld1", "ld2", "out0", "out1", "out2",
                     "pe", "dve", "act", "gp"]
        semh = {n: en(nc.semaphore(n)) for n in sem_names}

        t["psum_free"] = {}
        e1 = Em()
        emit(e1, t)
        t["psum_free"] = compute_psum_free(e1.marks)
        e1b = Em()
        emit(e1b, t)
        assert e1b.marks == e1.marks
        e2 = Em(marks=e1b.marks, marksem=e1b.marksem)
        emit(e2, t)
        assert e2.cnt == e1b.cnt

        for name, streams in e2.phases:
            with nc.Block() as block:
                for eng_name in ENGINES:
                    items = streams[eng_name]
                    if not items:
                        continue

                    def body(E, items=items):
                        for it in items:
                            if it[0] == "op":
                                _, fn, sem, k = it
                                ins = fn(E)
                                if sem is not None:
                                    ins.then_inc(semh[sem], k)
                            else:
                                _, sem, v = it
                                E.wait_ge(semh[sem], v)

                    getattr(block, eng_name)(body)

    return nc


def kernel(L0_attn_x, L0_attn_gamma, L0_attn_beta, L0_attn_W_in, L0_attn_W_out,
           L0_attn_dt, L0_ffn_weight, L0_ffn_weight_scale, L0_ffn_input_scale):
    X = np.ascontiguousarray(np.asarray(L0_attn_x).reshape(16384, D))
    gamma = np.ascontiguousarray(
        np.broadcast_to(np.asarray(L0_attn_gamma, np.float32).reshape(1, D), (128, D)))
    beta = np.ascontiguousarray(
        np.broadcast_to(np.asarray(L0_attn_beta, np.float32).reshape(1, D), (128, D)))
    win = np.ascontiguousarray(np.asarray(L0_attn_W_in, np.float32))
    wout = np.ascontiguousarray(np.asarray(L0_attn_W_out, np.float32))
    dtv = np.asarray(L0_attn_dt, np.float32).reshape(E)
    dtrep = np.ascontiguousarray(np.broadcast_to(dtv.reshape(1, E), (128, E)))
    dtcol = np.ascontiguousarray(dtv.reshape(32, 128).T)
    w8 = np.asarray(L0_ffn_weight).view(ml_dtypes.float8_e4m3)
    scol = np.ascontiguousarray(
        np.broadcast_to(np.asarray(L0_ffn_input_scale).reshape(1, 1), (128, 1)))
    wscol = np.ascontiguousarray(
        np.broadcast_to(np.asarray(L0_ffn_weight_scale, np.float32).reshape(1, 1),
                        (128, 1)))

    nc = build_program()
    in_maps = []
    for c in range(NCORE):
        in_maps.append(dict(
            x=np.ascontiguousarray(X[c * TCORE:(c + 1) * TCORE]),
            grep=gamma, brep=beta, win=win, wout=wout, w8=w8,
            dtrep=dtrep, dtcol=dtcol, scol=scol, wscol=wscol))
    res = run_bass_kernel_spmd(nc, in_maps, core_ids=list(range(NCORE)))
    full = np.concatenate([np.asarray(r["out"]) for r in res.results], axis=0)
    return full.reshape(4, 4096, D).astype(np.float32, copy=False)
